# version 05
# baseline (speedup 1.0000x reference)
import math
import contextlib
import numpy as np
import concourse.bass as bass
import concourse.mybir as mybir
from concourse.bass_utils import run_bass_kernel_spmd

F32 = mybir.dt.float32
BF16 = mybir.dt.bfloat16
AF = mybir.ActivationFunctionType
ALU = mybir.AluOpType
AX = mybir.AxisListType
ET = mybir.EngineType

D = 2048
NB = 2
T = NB * 2048
NK = 16
NT = 32
NEXP = 32
DFF = 768
GU = 1536
ISCALE = 1.0 / math.sqrt(128.0)
NSW = 4
SWD = 512

C_ISCALE, C_SRMS_EPS, C_NEGH, C_EPS_LN, C_SRMS_LNC = 0, 1, 2, 3, 4
NCONST = 8
CENG = [ET.PE, ET.DVE, ET.Activation]

DEBUG_PHASE = "ALL"


class Chan:
    def __init__(self, st, nc, name, step=1):
        self.sem = st.enter_context(nc.semaphore(name))
        self.step = step
        self.count = 0
        self.name = name


_PHASE_SEQ = [0]


class Phase:
    def __init__(self, nc, chan_eng):
        self.nc = nc
        self.chan_eng = dict(chan_eng)
        self.chans = list(chan_eng)
        self.dry = True
        self.base = {}
        self.per = {}
        self.labels = {}
        self.i = None
        self.wrec = []
        self.wseq = {}
        self.wcur = {}
        self.wreg = {}
        self.winfo = {}

    def begin(self):
        self.dry = True
        self.base = {c: c.count for c in self.chans}
        self.labels = {}
        self.wrec = []

    def op(self, ch, fn, label=None):
        ch.count += ch.step
        if label is not None:
            self.labels[label] = (ch, ch.count - self.base[ch])
        if not self.dry:
            fn().then_inc(ch.sem, ch.step)

    def raw(self, fn):
        if not self.dry:
            fn()

    def _wkey(self, eng, ch):
        return (eng.engine, ch.name)

    def wait(self, eng, label, dit=0):
        if self.dry:
            self.wrec.append((eng, None, "lbl", (label, dit)))
        else:
            self._emit_wait(eng)

    def wait_self(self, eng, ch):
        if self.dry:
            rel = ch.count - self.base[ch]
            self.wrec.append((eng, ch, "self", rel))
        else:
            self._emit_wait(eng)

    def wait_ext(self, eng, ch, per, off, base_list):
        if self.dry:
            self.wrec.append((eng, ch, "ext", (per, off, base_list)))
        else:
            self._emit_wait(eng)

    def _emit_wait(self, eng):
        key = eng.engine
        seq = self.wemit[key]
        cur = self.wcurs[key]
        self.wcurs[key] = cur + 1
        item = seq[cur]
        if item is None:
            return
        ch, reg, delta = item
        if delta != 0:
            eng.reg_add(reg, reg, delta)
        eng.wait_ge(ch.sem, reg)

    def mid(self):
        self.per = {c: c.count - self.base[c] for c in self.chans}
        for c in self.chans:
            c.count = self.base[c]
        negoffs = {c: set() for c in self.chans}
        for (eng, ch, kind, payload) in self.wrec:
            if kind == "lbl":
                label, dit = payload
                lch, idx = self.labels[label]
                if lch in negoffs:
                    off = dit * self.per[lch] + idx
                    if off <= 0:
                        negoffs[lch].add(off)
            elif kind == "self":
                pass
        for c in self.chans:
            if self.per[c] <= 0:
                continue
            pre = c.count
            stops = sorted(pre + self.per[c] + o for o in negoffs[c]
                           if pre + self.per[c] + o > pre)
            for v in stops:
                if v > c.count:
                    self.chan_eng[c].sem_inc(c.sem, v - c.count)
                    c.count = v
            final = pre + self.per[c]
            if final > c.count:
                self.chan_eng[c].sem_inc(c.sem, final - c.count)
                c.count = final
        self.base = {c: c.count for c in self.chans}
        self.dry = False

    def prepare(self, st):
        self._owners = []
        resolved = []
        for (eng, ch, kind, payload) in self.wrec:
            if kind == "lbl":
                label, dit = payload
                lch, idx = self.labels[label]
                resolved.append((eng, lch, self.per[lch], self.base[lch],
                                 dit * self.per[lch] + idx))
            elif kind == "self":
                rel = payload
                resolved.append((eng, ch, self.per[ch], self.base[ch], rel))
            else:
                per, off, base_list = payload
                resolved.append((eng, ch, per, base_list[0], off))
        lastoff = {}
        o_max = {}
        per_engine = {}
        for (eng, ch, per, base, off) in resolved:
            k = (eng.engine, ch.name)
            o_max[k] = max(o_max.get(k, -10 ** 9), off)
        self.wemit = {}
        self.wcurs = {}
        regs = {}
        for (eng, ch, per, base, off) in resolved:
            k = (eng.engine, ch.name)
            ek = eng.engine
            self.wemit.setdefault(ek, [])
            self.wcurs[ek] = 0
            if off <= o_max[k] - per:
                self.wemit[ek].append(None)
                continue
            prev = lastoff.get(k)
            if prev is not None and off <= prev:
                self.wemit[ek].append(None)
                continue
            if k not in regs:
                _PHASE_SEQ[0] += 1
                reg = eng.alloc_register(f"w{_PHASE_SEQ[0]}_{ch.name}")
                self._owners.append((eng, reg))
                init = base + o_max[k] - per
                assert init >= 0, f"reg init negative for {k}: {init}"
                eng.reg_mov(reg, init)
                regs[k] = reg
                delta = per + off - o_max[k]
            else:
                reg = regs[k]
                delta = off - lastoff[k]
            assert delta > 0
            self.wemit[ek].append((ch, reg, delta))
            lastoff[k] = off

    def finish(self, n_iters):
        for c in self.chans:
            c.count = self.base[c] + n_iters * self.per[c]
        for eng, reg in self._owners:
            eng.free_register(reg)
        self._owners = []

    def val(self, label, it):
        ch, idx = self.labels[label]
        return max(0, self.base[ch] + it * self.per[ch] + idx)


def build_nc(debug_phase="ALL"):
    nc = bass.Bass("TRN2", target_bir_lowering=False)
    st = contextlib.ExitStack()
    st.__enter__()

    EIn = "ExternalInput"
    xin = nc.dram_tensor("xin", [T, D], F32, kind=EIn).ap()
    wqT = nc.dram_tensor("wqT", [D, D], F32, kind=EIn).ap()
    wkT = nc.dram_tensor("wkT", [D, D], F32, kind=EIn).ap()
    wvT = nc.dram_tensor("wvT", [D, D], F32, kind=EIn).ap()
    wuT = nc.dram_tensor("wuT", [D, D], F32, kind=EIn).ap()
    woT = nc.dram_tensor("woT", [D, D], F32, kind=EIn).ap()
    gateTd = nc.dram_tensor("gateTd", [D, NEXP], F32, kind=EIn).ap()
    wguTd = nc.dram_tensor("wguTd", [NEXP, D, GU], F32, kind=EIn).ap()
    wdTd = nc.dram_tensor("wdTd", [NEXP, DFF, D], F32, kind=EIn).ap()
    simin = nc.dram_tensor("simin", [32, 32], F32, kind=EIn).ap()
    i32in = nc.dram_tensor("i32in", [32, 32], F32, kind=EIn).ap()
    constsin = nc.dram_tensor("constsin", [128, NCONST], F32, kind=EIn).ap()
    identbin = nc.dram_tensor("identbin", [128, 128], F32, kind=EIn).ap()
    identfin = nc.dram_tensor("identfin", [128, 128], F32, kind=EIn).ap()
    onesfin = nc.dram_tensor("onesfin", [1, 128], F32, kind=EIn).ap()

    out = nc.dram_tensor("out", [T, D], F32, kind="ExternalOutput").ap()

    dbgA = debug_phase == "A"
    dbgC = debug_phase == "C"
    ot_hbm = nc.dram_tensor("ot_hbm", [NB, D, 2048], BF16, kind="Internal").ap()
    x1f = nc.dram_tensor("x1f", [T, D], F32,
                         kind="ExternalOutput" if dbgA else "Internal").ap()
    x1t = nc.dram_tensor("x1t", [D, T], BF16, kind="Internal").ap()
    cwTd = nc.dram_tensor("cwTd", [NEXP, T], F32,
                          kind="ExternalOutput" if dbgA else "Internal").ap()
    simfl = nc.dram_tensor("simfl", [32, 32], F32, kind="Internal").ap()
    simflT = nc.dram_tensor("simflT", [32, 32], F32, kind="Internal").ap()
    sink = nc.dram_tensor("sink", [16, 16], F32, kind="Internal").ap()
    wd_b = nc.dram_tensor("wd_b", [NEXP, DFF, D], BF16, kind="Internal").ap()
    hbuf = nc.dram_tensor("hbuf", [NEXP, DFF, T], BF16,
                          kind="ExternalOutput" if dbgC else "Internal").ap()
    rres = nc.dram_tensor("rres", [T, D], F32, kind="Internal").ap()

    def sbt(name, shape, dt):
        return st.enter_context(nc.sbuf_tensor(name, shape, dt)).ap()

    cst = sbt("cst", [128, NCONST], F32)
    idb = sbt("idb", [128, 128], BF16)
    idf = sbt("idf", [128, 128], F32)
    ones1 = sbt("ones1", [1, 128], F32)
    gateb = sbt("gateb", [128, NK * NEXP], BF16)
    sim32 = sbt("sim32", [32, 32], F32)
    simt1 = sbt("simt1", [32, 32], F32)
    i32s = sbt("i32s", [32, 32], F32)
    simrow = sbt("simrow", [1, 1024], F32)
    simB = sbt("simB", [128, 1024], F32)
    simBE = sbt("simBE", [128, 1024], F32)
    simt2 = sbt("simt2", [32, 32], F32)
    candb = sbt("candb", [128, 1024], F32)
    cwrow = sbt("cwrow", [1, T], F32)
    statm = sbt("statm", [128, NT], F32)
    stats = sbt("stats", [128, NT], F32)
    rr = {}
    for nmr in ("Lg", "prob", "rnk", "gt", "top8", "prim", "sec", "wgt",
                "C2", "keep", "acc", "t3", "t5", "cw", "cw1", "Mx"):
        rr[nmr] = sbt("r_" + nmr, [128, 32], F32)
    cwP = [rr["cw"], rr["cw1"]]
    c1 = sbt("c1", [128, 1], F32)
    c2 = sbt("c2", [128, 1], F32)
    ssb = sbt("ssb", [128, 32], F32)
    ssc = sbt("ssc", [128, 32], F32)
    lnm = sbt("lnm", [128, 2], F32)
    lnv = sbt("lnv", [128, 2], F32)
    lnr = sbt("lnr", [128, 2], F32)
    lnb = sbt("lnb", [128, 2], F32)

    ARN = 43616
    ARf = st.enter_context(nc.sbuf_tensor("AR", [128, ARN], F32)).ap()
    ARb = ARf.bitcast(BF16)

    def fvv(off, n):
        return ARf[:, off:off + n]

    def bvv(off, nb):
        return ARb[:, 2 * off:2 * off + nb]

    class Ar:
        def __init__(self, start=0):
            self.top = start
        def alloc(self, n):
            o = (self.top + 15) // 16 * 16
            self.top = o + n
            assert self.top <= ARN, f"arena overflow {self.top}"
            return o

    arA = Ar()
    o_xTb = arA.alloc(16384)
    o_KQ = arA.alloc(4096)
    o_VU = arA.alloc(4096)
    o_QT = arA.alloc(4096)
    o_sl = [arA.alloc(4096) for _ in range(2)]
    o_KV = arA.alloc(256)
    o_Ob = arA.alloc(2048)
    o_Of = [arA.alloc(256) for _ in range(2)]
    o_osq = [arA.alloc(512) for _ in range(2)]
    o_OTst = [arA.alloc(1024) for _ in range(2)]
    o_xtm = [o_KQ, o_KQ + 1024]

    arB = Ar()
    o_wo = arB.alloc(16384)
    o_OTs = [arB.alloc(1024) for _ in range(2)]
    o_xres = [arB.alloc(2048) for _ in range(2)]
    o_r1 = [arB.alloc(2048) for _ in range(2)]
    o_sq = [arB.alloc(2048) for _ in range(2)]
    o_x1 = [arB.alloc(2048) for _ in range(2)]
    o_x1b = [arB.alloc(1024) for _ in range(2)]
    o_x1ts = [arB.alloc(1024) for _ in range(2)]
    o_cwt = [arB.alloc(128) for _ in range(2)]

    arC = Ar()
    o_wgu = arC.alloc(2 * 12288)
    o_x1c = arC.alloc(2 * 4096)
    o_hst = [arC.alloc(1536) for _ in range(2)]
    o_sg = [arC.alloc(512) for _ in range(2)]
    o_t1 = [arC.alloc(512) for _ in range(2)]
    o_wdc = [arC.alloc(1536) for _ in range(2)]

    arD = Ar()
    o_y = [arD.alloc(4096) for _ in range(4)]
    o_hc = arD.alloc(2 * 1536)
    o_wds = arD.alloc(2 * 1536)
    o_x1ft = [arD.alloc(512) for _ in range(2)]
    o_rst = [arD.alloc(512) for _ in range(2)]
    o_sqd = [arD.alloc(512) for _ in range(2)]

    arE = Ar(arD.top)
    o_rrE = [arE.alloc(2048) for _ in range(2)]
    o_oe = [arE.alloc(2048) for _ in range(2)]

    pf = [st.enter_context(nc.psum_tensor(f"pf{i}", [128, 512], F32)).ap()
          for i in range(6)]
    pbt = st.enter_context(nc.psum_tensor("pbt", [128, 1024], BF16)).ap()

    ALL_CH = []

    def CH(name, step=1):
        c = Chan(st, nc, name, step)
        ALL_CH.append(c)
        return c

    c_pre = CH("c_pre", 16)
    c_preg = CH("c_preg", 16)
    c_prev = CH("c_prev")
    c_xd = [CH("c_xd0", 16), CH("c_xd1", 16)]
    c_a0t = CH("c_a0t")
    c_a0v = CH("c_a0v")
    c_slK = CH("c_slK", 16)
    c_slV = CH("c_slV", 16)
    c_slQ = CH("c_slQ", 16)
    c_slU = CH("c_slU", 16)
    c_amm = CH("c_amm")
    c_asc = CH("c_asc")
    c_avv = CH("c_avv")
    c_aod = [CH("c_aod0", 16), CH("c_aod1", 16)]
    c_ots = [CH("c_ots0", 16), CH("c_ots1", 16)]
    c_xrs = [CH("c_xrs0", 16), CH("c_xrs1", 16)]
    c_bmm = CH("c_bmm")
    c_bv = CH("c_bv")
    c_bsc = CH("c_bsc")
    c_bx1f = [CH("c_bx1f0", 16), CH("c_bx1f1", 16)]
    c_bx1t = [CH("c_bx1t0", 16), CH("c_bx1t1", 16)]
    c_bcwt = [CH("c_bcwt0", 16), CH("c_bcwt1", 16)]
    c_crow = CH("c_crow", 16)
    c_cx1 = [CH("c_cx10", 16), CH("c_cx11", 16)]
    c_cgd = [CH("c_cgd0", 16), CH("c_cgd1", 16)]
    c_cmm = CH("c_cmm")
    c_csc = CH("c_csc")
    c_cv = CH("c_cv")
    c_chd = [CH("c_chd0", 16), CH("c_chd1", 16)]
    c_cwc = CH("c_cwc", 16)
    c_dh = [CH("c_dh0", 16), CH("c_dh1", 16)]
    c_dxf = [CH("c_dxf0", 16), CH("c_dxf1", 16)]
    c_dgd = [CH("c_dgd0", 16), CH("c_dgd1", 16)]
    c_dmm = CH("c_dmm")
    c_dv = CH("c_dv")
    c_drd = [CH("c_drd0", 16), CH("c_drd1", 16)]
    c_ed = [CH("c_ed0", 16), CH("c_ed1", 16)]
    c_ev = CH("c_ev")
    c_esc = CH("c_esc")
    c_out = [CH("c_out0", 16), CH("c_out1", 16)]

    te, ve, sc, sy, gp = nc.tensor, nc.vector, nc.scalar, nc.sync, nc.gpsimd

    def drain_all_dma():
        for chd_ in ALL_CH:
            if chd_.step == 16 and chd_.count > 0:
                sy.wait_ge(chd_.sem, chd_.count)

    xTb = bvv(o_xTb, 32768).rearrange("p (k t) -> p k t", k=NK)
    KQ = bvv(o_KQ, 8192).rearrange("p (t e) -> p t e", t=16)
    VU = bvv(o_VU, 8192).rearrange("p (t e) -> p t e", t=16)
    QT = bvv(o_QT, 8192).rearrange("p (h t) -> p h t", h=4)
    SL0 = bvv(o_sl[0], 8192).rearrange("p (k e) -> p k e", k=NK)
    SL1 = bvv(o_sl[1], 8192).rearrange("p (k e) -> p k e", k=NK)
    KVb = bvv(o_KV, 512)
    Obb2 = [bvv(o_Ob, 4096)[:, 0:2048], bvv(o_Ob, 4096)[:, 2048:4096]]
    WOs = bvv(o_wo, 32768).rearrange("p (k e) -> p k e", k=NK)
    GTB = gateb.rearrange("p (k e) -> p k e", k=NK)
    WGU = [bvv(o_wgu + p * 12288, 24576).rearrange("p (k m) -> p k m", k=NK)
           for p in range(2)]
    X1C = [bvv(o_x1c + p * 4096, 8192).rearrange("p (k n) -> p k n", k=NK)
           for p in range(2)]
    HST = [bvv(o_hst[p], 3072).rearrange("p (m n) -> p m n", m=6)
           for p in range(2)]
    HC = [bvv(o_hc + p * 1536, 3072).rearrange("p (k n) -> p k n", k=6)
          for p in range(2)]
    WDS = [bvv(o_wds + p * 1536, 3072).rearrange("p (k m) -> p k m", k=6)
           for p in range(2)]

    sy.dma_start(cst[:, :], constsin).then_inc(c_pre.sem, 16)
    gp.dma_start(idb[:, :], identbin).then_inc(c_preg.sem, 16)
    sy.dma_start(idf[:, :], identfin).then_inc(c_pre.sem, 16)
    sy.dma_start(ones1[:, :], onesfin).then_inc(c_pre.sem, 16)
    sy.dma_start(sim32[:, :], simin).then_inc(c_pre.sem, 16)
    sy.dma_start(i32s[:, :], i32in).then_inc(c_pre.sem, 16)
    gp.dma_start(gateb[:, :].rearrange("p (k e) -> p k e", k=NK),
                 gateTd.rearrange("(k p) e -> p k e", p=128)).then_inc(c_preg.sem, 16)
    c_preg.count = 32
    c_pre.count = 80
    ve.wait_ge(c_pre.sem, 80)
    ve.tensor_tensor(simt1[:, :], sim32[:, :], i32s[:, :], ALU.mult).then_inc(c_prev.sem, 1)
    ve.wait_ge(c_prev.sem, 1)
    ve.tensor_tensor(simt1[:, :], sim32[:, :], simt1[:, :], ALU.subtract).then_inc(c_prev.sem, 1)
    ve.wait_ge(c_prev.sem, 2)
    ve.tensor_tensor(simt1[:, :], simt1[:, :], i32s[:, :], ALU.add).then_inc(c_prev.sem, 1)
    ve.memset(statm[:, :], 0.0).then_inc(c_prev.sem, 1)
    ve.memset(stats[:, :], 0.0).then_inc(c_prev.sem, 1)
    ve.memset(rr["cw"][:, :], 0.0).then_inc(c_prev.sem, 1)
    ve.memset(rr["cw1"][:, :], 0.0).then_inc(c_prev.sem, 1)
    c_prev.count = 7
    sy.wait_ge(c_prev.sem, 3)
    sy.dma_start(simfl, simt1[:, :]).then_inc(c_pre.sem, 16)
    c_pre.count += 16
    sy.wait_ge(c_pre.sem, c_pre.count)
    sy.dma_start(simrow[:, :],
                 simfl.rearrange("s e -> (s e)").rearrange("(o q) -> o q", o=1)
                 ).then_inc(c_pre.sem, 16)
    c_pre.count += 16
    te.wait_ge(c_prev.sem, 3)
    te.wait_ge(c_pre.sem, 80)
    te.transpose(pf[5][0:32, 0:32], simt1[:, :], idf[0:32, 0:32]).then_inc(c_amm.sem, 1)
    c_amm.count = 1
    ve.wait_ge(c_amm.sem, 1)
    ve.tensor_copy(simt2[:, :], pf[5][0:32, 0:32]).then_inc(c_prev.sem, 1)
    c_prev.count = 8
    sy.wait_ge(c_prev.sem, 8)
    sy.dma_start(simflT, simt2[:, :]).then_inc(c_pre.sem, 16)
    c_pre.count += 16
    te.wait_ge(c_pre.sem, c_pre.count)
    te.matmul(pf[0][:, :], ones1[:, :], simrow[:, 0:512],
              start=True, stop=True).then_inc(c_amm.sem, 1)
    te.matmul(pf[1][:, :], ones1[:, :], simrow[:, 512:1024],
              start=True, stop=True).then_inc(c_amm.sem, 1)
    c_amm.count = 3
    ve.wait_ge(c_amm.sem, 3)
    ve.tensor_copy(simB[:, 0:512], pf[0][:, :]).then_inc(c_prev.sem, 1)
    ve.tensor_copy(simB[:, 512:1024], pf[1][:, :]).then_inc(c_prev.sem, 1)
    c_prev.count = 10
    sy.wait_ge(c_amm.sem, 3)
    sy.dma_start(simrow[:, :],
                 simflT.rearrange("s e -> (s e)").rearrange("(o q) -> o q", o=1)
                 ).then_inc(c_pre.sem, 16)
    c_pre.count += 16
    te.wait_ge(c_pre.sem, c_pre.count)
    te.matmul(pf[2][:, :], ones1[:, :], simrow[:, 0:512],
              start=True, stop=True).then_inc(c_amm.sem, 1)
    te.matmul(pf[3][:, :], ones1[:, :], simrow[:, 512:1024],
              start=True, stop=True).then_inc(c_amm.sem, 1)
    c_amm.count = 5
    ve.wait_ge(c_amm.sem, 5)
    ve.tensor_copy(simBE[:, 0:512], pf[2][:, :]).then_inc(c_prev.sem, 1)
    ve.tensor_copy(simBE[:, 512:1024], pf[3][:, :]).then_inc(c_prev.sem, 1)
    c_prev.count = 12
    for chi, chp in enumerate((c_aod[0], c_aod[1], c_bx1f[0], c_bx1f[1],
                               c_bx1t[0], c_bx1t[1], c_bcwt[0], c_bcwt[1],
                               c_bcwt[0], c_bcwt[1], c_chd[0], c_chd[1])):
        if chp.count:
            sy.wait_ge(chp.sem, chp.count)
        sy.dma_start(sink[chi:chi + 1, 0:8], cst[0:1, 0:8]).then_inc(chp.sem, 16)
        chp.count += 16
    nc.all_engine_barrier()
    if debug_phase == "PRE":
        sy.wait_ge(c_pre.sem, c_pre.count)
        drain_all_dma()
        st.close()
        return nc

    for b in range(NB if debug_phase not in ("A0", "A1") else 1):
        a0_xd = [c_xd[0].count, c_xd[1].count]
        a0_t, a0_v = c_a0t.count, c_a0v.count
        for i in range(16):
            tt = b * 16 + i
            if i >= 2:
                gp.wait_ge(c_a0t.sem, a0_t + (i - 1) * 16)
            gp.dma_start(bvv(o_xtm[i % 2], 2048),
                         xin[tt * 128:(tt + 1) * 128, :]).then_inc(c_xd[i % 2].sem, 16)
            c_xd[i % 2].count += 16
        te.wait_ge(c_preg.sem, 32)
        for i in range(16):
            te.wait_ge(c_xd[i % 2].sem, a0_xd[i % 2] + (i // 2 + 1) * 16)
            if i >= 1:
                te.wait_ge(c_a0v.sem, a0_v + i * 2)
            for k in range(16):
                if k == 8:
                    te.wait_ge(c_a0v.sem, a0_v + i * 2 + 1)
                te.transpose(pbt[:, (k % 8) * 128:(k % 8) * 128 + 128],
                             bvv(o_xtm[i % 2], 2048)[:, k * 128:(k + 1) * 128],
                             idb[:, :]).then_inc(c_a0t.sem, 1)
                c_a0t.count += 1
            ve.wait_ge(c_a0t.sem, a0_t + i * 16 + 8)
            ve.tensor_copy(xTb[:, 0:8, i * 128:(i + 1) * 128],
                           pbt[:, :].rearrange("p (h t) -> p h t", h=8)
                           ).then_inc(c_a0v.sem, 1)
            c_a0v.count += 1
            ve.wait_ge(c_a0t.sem, a0_t + i * 16 + 16)
            ve.tensor_copy(xTb[:, 8:16, i * 128:(i + 1) * 128],
                           pbt[:, :].rearrange("p (h t) -> p h t", h=8)
                           ).then_inc(c_a0v.sem, 1)
            c_a0v.count += 1
        nc.all_engine_barrier()
        if debug_phase == "A0":
            drain_all_dma()
            st.close()
            return nc

        ph = Phase(nc, {c_amm: te, c_asc: sc, c_avv: ve})
        slb0K = [0]
        slb0V = [0]
        slb0Q = [0]
        slb0U = [0]
        aod0 = [[0], [0]]

        def a1_body(P):
            def MMG(bank, lhsT_fn, rhs_fn, label):
                for k in range(16):
                    fn = (lambda k=k: te.matmul(bank, lhsT_fn(k), rhs_fn(k),
                                                start=(k == 0), stop=(k == 15)))
                    if k == 15:
                        P.op(c_amm, fn, label)
                    else:
                        P.raw(fn)
            P.wait_ext(te, c_slK, 16, 16, slb0K)
            for t in range(16):
                if t >= 2:
                    P.wait(te, f"evK{t-2}")
                else:
                    P.wait(te, f"evQ{14+t}", dit=-1)
                MMG(pf[t % 2][:, :],
                    lambda k, t=t: xTb[:, k, t * 128:(t + 1) * 128],
                    lambda k: SL0[:, k, :], f"mmK{t}")
            P.wait_ext(te, c_slV, 16, 16, slb0V)
            for t in range(16):
                if t >= 2:
                    P.wait(te, f"evV{t-2}")
                else:
                    P.wait(te, f"vObB{12+t}", dit=-1)
                MMG(pf[2 + t % 2][:, :],
                    lambda k, t=t: xTb[:, k, t * 128:(t + 1) * 128],
                    lambda k: SL1[:, k, :], f"mmV{t}")
            P.wait(te, "evK15")
            P.wait(te, "evV15")
            P.wait(te, "vObB15", dit=-1)
            for hh in range(4):
                MMG(pf[4][:, hh * 128:(hh + 1) * 128],
                    lambda lt, hh=hh: KQ[:, lt, hh * 128:(hh + 1) * 128],
                    lambda lt, hh=hh: VU[:, lt, hh * 128:(hh + 1) * 128],
                    f"mmKV{hh}")
            P.wait_ext(te, c_slQ, 16, 16, slb0Q)
            for qg in range(16):
                hh, n = divmod(qg, 4)
                if qg >= 2:
                    P.wait(te, f"evQ{qg-2}")
                MMG(pf[qg % 2][:, :],
                    lambda k, hh=hh: SL0[:, k, hh * 128:(hh + 1) * 128],
                    lambda k, n=n: xTb[:, k, n * 512:(n + 1) * 512], f"mmQ{qg}")
            P.wait_ext(te, c_slU, 16, 16, slb0U)
            for t in range(16):
                if t >= 2:
                    P.wait(te, f"evU{t-2}")
                MMG(pf[2 + t % 2][:, :],
                    lambda k, t=t: xTb[:, k, t * 128:(t + 1) * 128],
                    lambda k: SL1[:, k, :], f"mmU{t}")
            P.wait(te, "evKV")
            P.wait(te, "evQ15")
            P.wait(te, "evU15")
            for j in range(5):
                if j < 4:
                    hh = j
                    for t in range(16):
                        oc = hh * 16 + t
                        qb = oc // 4
                        bank = pf[2 + qb % 4]
                        col = (oc % 4) * 128
                        if t % 4 == 0:
                            if qb >= 4:
                                P.wait(te, f"vObB{qb-4}")
                            else:
                                P.wait(te, f"vObB{qb+12}", dit=-1)
                        P.op(c_amm, (lambda hh=hh, t=t, bank=bank, col=col:
                                     te.matmul(
                            bank[:, col:col + 128],
                            QT[:, hh, t * 128:(t + 1) * 128],
                            KVb[:, hh * 128:(hh + 1) * 128],
                            start=True, stop=True)), f"mmO{oc}")
                if j >= 1:
                    hh = j - 1
                    for t in range(16):
                        oc = hh * 16 + t
                        qb = oc // 4
                        if t % 4 == 0:
                            P.wait(te, f"vOfB{qb}")
                        if oc % 8 == 0:
                            if oc == 0:
                                P.wait(te, "vOTb7", dit=-1)
                            else:
                                P.wait(te, f"vOTb{oc // 8 - 1}")
                        P.op(c_amm, (lambda t=t, qb=qb: te.transpose(
                            pbt[:, (t % 8) * 128:(t % 8) * 128 + 128],
                            bvv(o_Of[qb % 2], 512)[:, (t % 4) * 128:(t % 4) * 128 + 128],
                            idb[:, :])), f"mmT{oc}")

            for t in range(16):
                P.wait(sc, f"mmK{t}")
                P.op(c_asc, (lambda t=t: sc.activation(
                    KQ[:, t, :], pf[t % 2][:, :], AF.Relu,
                    scale=cst[:, C_ISCALE:C_ISCALE + 1])), f"evK{t}")
            for qg in range(16):
                hh, n = divmod(qg, 4)
                P.wait(sc, f"mmQ{qg}")
                P.op(c_asc, (lambda hh=hh, n=n, qg=qg: sc.activation(
                    QT[:, hh, n * 512:(n + 1) * 512], pf[qg % 2][:, :],
                    AF.Relu, scale=cst[:, C_ISCALE:C_ISCALE + 1])), f"evQ{qg}")
            for t in range(16):
                P.wait(sc, f"mmU{t}")
                P.op(c_asc, (lambda t=t: sc.activation(
                    VU[:, t, :], pf[2 + t % 2][:, :], AF.Silu)), f"evU{t}")
            for hh in range(4):
                half = (hh % 2) * 16
                P.wait(sc, f"vSS{hh}")
                if hh >= 2:
                    P.wait(sc, f"vOfB{(hh - 2) * 4 + 3}")
                else:
                    P.wait(sc, f"vOfB{(hh + 2) * 4 + 3}", dit=-1)
                P.op(c_asc, (lambda half=half: sc.activation(
                    ssc[:, half:half + 16], ssb[:, half:half + 16], AF.Ln,
                    bias=cst[:, C_SRMS_EPS:C_SRMS_EPS + 1])), f"scL{hh}")
                P.wait(sc, f"scL{hh}")
                P.op(c_asc, (lambda half=half: sc.activation(
                    ssc[:, half:half + 16], ssc[:, half:half + 16], AF.Exp,
                    scale=cst[:, C_NEGH:C_NEGH + 1],
                    bias=cst[:, C_SRMS_LNC:C_SRMS_LNC + 1])), f"scE{hh}")

            for t in range(16):
                P.wait(ve, f"mmV{t}")
                P.op(c_avv, (lambda t=t: ve.tensor_copy(
                    VU[:, t, :], pf[2 + t % 2][:, :])), f"evV{t}")
            P.wait(ve, "mmKV3")
            P.op(c_avv, (lambda: ve.tensor_copy(KVb[:, :], pf[4][:, :])), "evKV")
            for j in range(5):
                if j < 4:
                    hh = j
                    half = (hh % 2) * 16
                    for tb in range(4):
                        oc0 = hh * 16 + tb * 4
                        qb = oc0 // 4
                        bank = pf[2 + qb % 4]
                        P.wait(ve, f"mmO{oc0 + 3}")
                        if tb == 0:
                            if hh >= 2:
                                P.wait(ve, f"scL{hh-2}")
                            else:
                                P.wait(ve, f"scL{hh+2}", dit=-1)
                        P.op(c_avv, (lambda tb=tb, bank=bank, hp=hh % 2:
                                     ve.tensor_copy(
                            Obb2[hp][:, tb * 512:(tb + 1) * 512],
                            bank[:, :])), f"vObB{qb}")
                        P.wait_self(ve, c_avv)
                        P.op(c_avv, (lambda tb=tb, p=tb % 2, hp=hh % 2:
                                     ve.tensor_tensor(
                            fvv(o_osq[p], 512),
                            Obb2[hp][:, tb * 512:(tb + 1) * 512],
                            Obb2[hp][:, tb * 512:(tb + 1) * 512], ALU.mult)),
                            None)
                        P.wait_self(ve, c_avv)
                        P.op(c_avv, (lambda tb=tb, p=tb % 2, half=half:
                                     ve.reduce_sum(
                            bass.AP(ssb.tensor, half + tb * 4,
                                    [[32, 128], [1, 4]]),
                            bass.AP(ARf.tensor, o_osq[p],
                                    [[ARN, 128], [128, 4], [1, 128]]),
                            AX.X)), f"vSS{hh}" if tb == 3 else None)
                if j >= 1:
                    hh = j - 1
                    half = (hh % 2) * 16
                    for tb in range(4):
                        oc0 = hh * 16 + tb * 4
                        qb = oc0 // 4
                        if tb == 0:
                            P.wait(ve, f"scE{hh}")
                        if qb >= 2:
                            P.wait(ve, f"mmT{(qb - 2) * 4 + 3}")
                        else:
                            P.wait(ve, f"mmT{(qb + 14) * 4 + 3}", dit=-1)
                        sscrep = bass.AP(ssc.tensor, half + tb * 4,
                                         [[32, 128], [1, 4], [0, 128]])
                        ofb3 = bass.AP(ARb.tensor, 2 * o_Of[qb % 2],
                                       [[2 * ARN, 128], [128, 4], [1, 128]])
                        obbin = bass.AP(ARb.tensor,
                                        2 * o_Ob + (hh % 2) * 2048 + tb * 512,
                                        [[2 * ARN, 128], [128, 4], [1, 128]])
                        vuin = bass.AP(ARb.tensor,
                                       2 * o_VU + tb * 4 * 512 + hh * 128,
                                       [[2 * ARN, 128], [512, 4], [1, 128]])
                        P.op(c_avv, (lambda ofb3=ofb3, obbin=obbin,
                                     sscrep=sscrep: ve.tensor_tensor(
                            ofb3, obbin, sscrep, ALU.mult)), None)
                        P.wait_self(ve, c_avv)
                        P.op(c_avv, (lambda ofb3=ofb3, vuin=vuin:
                                     ve.tensor_tensor(
                            ofb3, ofb3, vuin, ALU.mult)), f"vOfB{qb}")
                        if tb == 1 or tb == 3:
                            halfi = 0 if tb == 1 else 1
                            P.wait(ve, f"mmT{hh * 16 + (halfi + 1) * 8 - 1}")
                            if halfi == 0:
                                P.wait_ext(ve, c_aod[hh % 2], 32,
                                           16 if hh >= 2 else 0,
                                           aod0[hh % 2])
                            P.op(c_avv, (lambda hh=hh, halfi=halfi:
                                         ve.tensor_copy(
                                bvv(o_OTst[hh % 2], 2048)
                                [:, halfi * 1024:(halfi + 1) * 1024],
                                pbt[:, :])), f"vOTb{hh * 2 + halfi}")

        ph.begin()
        a1_body(ph)
        ph.mid()
        aod0[0][0] = c_aod[0].count
        aod0[1][0] = c_aod[1].count
        slb0K[0] = c_slK.count
        slb0V[0] = c_slV.count
        slb0Q[0] = c_slQ.count
        slb0U[0] = c_slU.count
        for g in range(4):
            if b + g > 0:
                gp.wait_ge(c_amm.sem, ph.val("mmQ15", g - 1))
            gp.dma_start(SL0, wkT.rearrange("(k p) e -> p k e", p=128)
                         [:, :, g * 512:(g + 1) * 512]).then_inc(c_slK.sem, 16)
            c_slK.count += 16
            if b + g > 0:
                gp.wait_ge(c_amm.sem, ph.val("mmU15", g - 1))
            gp.dma_start(SL1, wvT.rearrange("(k p) e -> p k e", p=128)
                         [:, :, g * 512:(g + 1) * 512]).then_inc(c_slV.sem, 16)
            c_slV.count += 16
            gp.wait_ge(c_amm.sem, ph.val("mmK15", g))
            gp.dma_start(SL0, wqT.rearrange("(k p) e -> p k e", p=128)
                         [:, :, g * 512:(g + 1) * 512]).then_inc(c_slQ.sem, 16)
            c_slQ.count += 16
            gp.wait_ge(c_amm.sem, ph.val("mmV15", g))
            gp.dma_start(SL1, wuT.rearrange("(k p) e -> p k e", p=128)
                         [:, :, g * 512:(g + 1) * 512]).then_inc(c_slU.sem, 16)
            c_slU.count += 16
            for hh in range(4):
                h = g * 4 + hh
                sy.wait_ge(c_avv.sem, ph.val(f"vOTb{hh * 2 + 1}", g))
                sy.dma_start(ot_hbm[b, h * 128:(h + 1) * 128, :],
                             bvv(o_OTst[hh % 2], 2048)).then_inc(c_aod[hh % 2].sem, 16)
                c_aod[hh % 2].count += 16
        ph.prepare(st)
        with nc.Fori(0, 4, engines=CENG) as gi:
            ph.i = gi
            a1_body(ph)
        ph.finish(4)
        nc.all_engine_barrier()
        if debug_phase == "A1":
            sy.wait_ge(c_aod[0].sem, c_aod[0].count)
            sy.wait_ge(c_aod[1].sem, c_aod[1].count)
            drain_all_dma()
            st.close()
            return nc

    ph2 = Phase(nc, {c_bmm: te, c_bv: ve, c_bsc: sc})
    ots0 = [[0], [0]]
    xrs0 = [[0], [0]]
    bx1f0 = [[0], [0]]
    bx1t0 = [[0], [0]]
    bcwt0 = [[0], [0]]

    def router(P, par):
        def V(fn, label=None, self_wait=True):
            if self_wait:
                P.wait_self(ve, c_bv)
            P.op(c_bv, fn, label)
        prob_h = rr["prob"].tensor
        wgt_h = rr["wgt"].tensor
        prim_h = rr["prim"].tensor
        Mx_h = rr["Mx"].tensor
        C2_h = rr["C2"].tensor
        p_qrep = bass.AP(prob_h, 0, [[32, 128], [1, 32], [0, 32]])
        p_erep = bass.AP(prob_h, 0, [[32, 128], [0, 32], [1, 32]])
        prim_srep = bass.AP(prim_h, 0, [[32, 128], [0, 32], [1, 32]])
        prim_emaj = bass.AP(prim_h, 0, [[32, 128], [1, 32], [0, 32]])
        mx_erep = bass.AP(Mx_h, 0, [[32, 128], [0, 32], [1, 32]])
        c2_erep = bass.AP(C2_h, 0, [[32, 128], [0, 32], [1, 32]])
        gt3 = bass.AP(candb.tensor, 0, [[1024, 128], [32, 32], [1, 32]])
        cd3 = bass.AP(candb.tensor, 0, [[1024, 128], [32, 32], [1, 32]])
        cdE3 = gt3
        simB3 = bass.AP(simB.tensor, 0, [[1024, 128], [32, 32], [1, 32]])
        simBE3 = bass.AP(simBE.tensor, 0, [[1024, 128], [32, 32], [1, 32]])

        V(lambda: ve.tensor_copy(rr["Lg"][:, :], pf[4][:, 0:32]),
          f"bvLG{par}", self_wait=False)
        V(lambda: ve.reduce_max(c1[:, :], rr["Lg"][:, :], AX.X))
        V(lambda: ve.tensor_scalar(rr["Lg"][:, :], rr["Lg"][:, :], c1[:, :],
                                   None, ALU.subtract), f"bvLS{par}")
        P.wait(ve, f"bscEX{par}")
        V(lambda: ve.tensor_tensor(gt3, p_erep, p_qrep, ALU.is_gt),
          self_wait=False)
        V(lambda: ve.reduce_sum(rr["rnk"][:, :], gt3, AX.X))
        V(lambda: ve.tensor_scalar(rr["top8"][:, :], rr["rnk"][:, :], 8.0,
                                   None, ALU.is_lt))
        V(lambda: ve.tensor_scalar(rr["prim"][:, :], rr["rnk"][:, :], 4.0,
                                   None, ALU.is_lt), self_wait=False)
        V(lambda: ve.tensor_tensor(rr["sec"][:, :], rr["top8"][:, :],
                                   rr["prim"][:, :], ALU.subtract))
        V(lambda: ve.tensor_tensor(rr["wgt"][:, :], rr["prob"][:, :],
                                   rr["top8"][:, :], ALU.mult), f"bvWG{par}")
        V(lambda: ve.reduce_sum(c1[:, :], rr["wgt"][:, :], AX.X))
        V(lambda: ve.tensor_scalar(c1[:, :], c1[:, :], 1e-12, None, ALU.max))
        V(lambda: ve.reciprocal(c2[:, :], c1[:, :]))
        V(lambda: ve.tensor_scalar(rr["wgt"][:, :], rr["wgt"][:, :], c2[:, :],
                                   None, ALU.mult))
        V(lambda: ve.tensor_tensor(cd3, prim_srep, simB3, ALU.mult))
        V(lambda: ve.reduce_max(rr["Mx"][:, :], cd3, AX.X))
        V(lambda: ve.tensor_scalar(rr["t3"][:, :], rr["Mx"][:, :], 0.5, None,
                                   ALU.is_ge))
        V(lambda: ve.tensor_scalar(rr["keep"][:, :], rr["t3"][:, :], 1.0, -1.0,
                                   ALU.subtract, ALU.mult))
        V(lambda: ve.tensor_tensor(rr["C2"][:, :], rr["wgt"][:, :],
                                   rr["sec"][:, :], ALU.mult), self_wait=False)
        V(lambda: ve.tensor_tensor(rr["t5"][:, :], rr["C2"][:, :],
                                   rr["keep"][:, :], ALU.mult))
        V(lambda: ve.tensor_tensor(rr["C2"][:, :], rr["C2"][:, :],
                                   rr["t3"][:, :], ALU.mult))
        V(lambda: ve.tensor_tensor(cdE3, prim_emaj, simBE3, ALU.mult))
        V(lambda: ve.tensor_tensor(gt3, cdE3, mx_erep, ALU.is_equal))
        V(lambda: ve.tensor_tensor(gt3, gt3, c2_erep, ALU.mult))
        V(lambda: ve.reduce_sum(rr["acc"][:, :], gt3, AX.X))
        P.wait(ve, f"bmmCWT{par}")
        V(lambda: ve.tensor_tensor(cwP[par][:, :], rr["wgt"][:, :],
                                   rr["prim"][:, :], ALU.mult))
        V(lambda: ve.tensor_tensor(cwP[par][:, :], cwP[par][:, :],
                                   rr["t5"][:, :], ALU.add))
        V(lambda: ve.tensor_tensor(cwP[par][:, :], cwP[par][:, :],
                                   rr["acc"][:, :], ALU.add), f"bvCW{par}")

    def a2_body(P):
        OTsl_ = [bvv(o_OTs[p], 2048).rearrange("p (k t) -> p k t", k=NK)
                 for p in range(2)]
        R1_ = [fvv(o_r1[p], 2048) for p in range(2)]
        SQ_ = [fvv(o_sq[p], 2048) for p in range(2)]
        X1_ = [fvv(o_x1[p], 2048) for p in range(2)]
        X1B_ = [bvv(o_x1b[p], 2048) for p in range(2)]
        X1TS_ = [bvv(o_x1ts[p], 2048) for p in range(2)]
        XR_ = [fvv(o_xres[p], 2048) for p in range(2)]
        CWT_ = [fvv(o_cwt[p], 128) for p in range(2)]

        for par in range(2):
            P.wait_ext(te, c_ots[par], 16, 16, ots0[par])
            for ec in range(4):
                gq = par * 4 + ec
                if gq >= 2:
                    P.wait(te, f"br1_{gq-2}")
                else:
                    P.wait(te, f"br1_{gq+6}", dit=-1)
                for k in range(16):
                    fn = (lambda ec=ec, k=k, gq=gq, par=par: te.matmul(
                        pf[gq % 2][:, :], OTsl_[par][:, k, :],
                        WOs[:, k, ec * 512:(ec + 1) * 512],
                        start=(k == 0), stop=(k == 15)))
                    if k == 15:
                        P.op(c_bmm, fn, f"bmmWO{gq}")
                    else:
                        P.raw(fn)
        def emit_tr(par):
            P.wait(te, f"bvX1B{par}")
            for k in range(16):
                if k == 8:
                    P.wait(te, f"bvTS{par}_0")
                elif k == 0:
                    if par == 0:
                        P.wait(te, "bvTS1_1", dit=-1)
                    else:
                        P.wait(te, "bvTS0_1")
                P.op(c_bmm, (lambda k=k, par=par: te.transpose(
                    pbt[:, (k % 8) * 128:(k % 8) * 128 + 128],
                    X1B_[par][:, k * 128:(k + 1) * 128], idb[:, :])),
                    f"bmmT{par}_{k}")

        def emit_lg(par):
            P.wait(te, f"bvTS{par}_1")
            if par == 0:
                P.wait(te, "bvLG1", dit=-1)
            else:
                P.wait(te, "bvLG0")
            for k in range(16):
                fn = (lambda k=k, par=par: te.matmul(
                    pf[4][:, 0:32], X1TS_[par][:, k * 128:(k + 1) * 128],
                    GTB[:, k, :], start=(k == 0), stop=(k == 15)))
                if k == 15:
                    P.op(c_bmm, fn, f"bmmLG{par}")
                else:
                    P.raw(fn)

        def emit_cwt(par):
            P.wait(te, f"bvCW{par}", dit=-1)
            if par == 0:
                P.wait(te, "bvCWT1", dit=-1)
            else:
                P.wait(te, "bvCWT0")
            P.op(c_bmm, (lambda par=par: te.transpose(
                pf[5][0:32, 0:128], cwP[par][:, :], idf[:, :])),
                f"bmmCWT{par}")

        emit_tr(0)
        emit_lg(0)
        emit_tr(1)
        emit_cwt(0)
        emit_cwt(1)
        emit_lg(1)

        for par in range(2):
            P.wait_ext(ve, c_xrs[par], 16, 16, xrs0[par])
            P.wait(ve, f"bscAP{par}", dit=-1)
            for ec in range(4):
                gq = par * 4 + ec
                P.wait(ve, f"bmmWO{gq}")
                P.op(c_bv, (lambda ec=ec, gq=gq, par=par: ve.tensor_tensor(
                    R1_[par][:, ec * 512:(ec + 1) * 512], pf[gq % 2][:, :],
                    XR_[par][:, ec * 512:(ec + 1) * 512], ALU.add)),
                    f"br1_{par*4+ec}")
            P.wait_self(ve, c_bv)
            P.op(c_bv, (lambda par=par: ve.reduce_sum(
                lnm[:, par:par + 1], R1_[par], AX.X)), None)
            P.wait_self(ve, c_bv)
            P.op(c_bv, (lambda par=par: ve.tensor_scalar(
                lnm[:, par:par + 1], lnm[:, par:par + 1], 1.0 / D, None,
                ALU.mult)), f"bvM{par}")
            P.op(c_bv, (lambda par=par: ve.tensor_tensor(
                SQ_[par], R1_[par], R1_[par], ALU.mult)), None)
            P.wait_self(ve, c_bv)
            P.op(c_bv, (lambda par=par: ve.reduce_sum(
                lnv[:, par:par + 1], SQ_[par], AX.X)), None)
            P.wait_self(ve, c_bv)
            P.op(c_bv, (lambda par=par: ve.tensor_scalar(
                lnv[:, par:par + 1], lnv[:, par:par + 1], 1.0 / D, None,
                ALU.mult)), None)
            P.op(c_bv, (lambda par=par: ve.tensor_tensor(
                c2[:, :], lnm[:, par:par + 1], lnm[:, par:par + 1], ALU.mult)),
                None)
            P.wait_self(ve, c_bv)
            P.op(c_bv, (lambda par=par: ve.tensor_tensor(
                lnv[:, par:par + 1], lnv[:, par:par + 1], c2[:, :],
                ALU.subtract)), f"bvVAR{par}")
            P.wait(ve, f"bscRS{par}")
            P.op(c_bv, (lambda par=par: ve.tensor_scalar(
                lnb[:, par:par + 1], lnm[:, par:par + 1], lnr[:, par:par + 1],
                -1.0, ALU.mult, ALU.mult)), f"bvNMR{par}")
            P.wait(ve, f"bscAP{par}")
            P.op(c_bv, (lambda par=par: ve.tensor_copy(X1B_[par], X1_[par])),
                 f"bvX1B{par}")
        for par in range(2):
            P.wait(ve, f"bmmT{par}_7")
            P.wait_ext(ve, c_bx1t[par], 16, 0, bx1t0[par])
            P.op(c_bv, (lambda par=par: ve.tensor_copy(
                X1TS_[par][:, 0:1024], pbt[:, :])), f"bvTS{par}_0")
            P.wait(ve, f"bmmT{par}_15")
            P.op(c_bv, (lambda par=par: ve.tensor_copy(
                X1TS_[par][:, 1024:2048], pbt[:, :])), f"bvTS{par}_1")
        for par in range(2):
            P.wait(ve, f"bmmCWT{par}")
            P.wait_ext(ve, c_bcwt[par], 16, -16, bcwt0[par])
            P.op(c_bv, (lambda par=par: ve.tensor_copy(
                CWT_[par][0:32, :], pf[5][0:32, 0:128])), f"bvCWT{par}")
        for par in range(2):
            P.wait(ve, f"bmmLG{par}")
            router(P, par)

        for par in range(2):
            P.wait(sc, f"bvVAR{par}")
            P.op(c_bsc, (lambda par=par: sc.activation(
                lnr[:, par:par + 1], lnv[:, par:par + 1], AF.Ln,
                bias=cst[:, C_EPS_LN:C_EPS_LN + 1])), None)
            P.wait_self(sc, c_bsc)
            P.op(c_bsc, (lambda par=par: sc.activation(
                lnr[:, par:par + 1], lnr[:, par:par + 1], AF.Exp,
                scale=cst[:, C_NEGH:C_NEGH + 1])), f"bscRS{par}")
            P.wait(sc, f"bvNMR{par}")
            P.wait_ext(sc, c_bx1f[par], 16, 0, bx1f0[par])
            P.op(c_bsc, (lambda par=par: sc.activation(
                X1_[par], R1_[par], AF.Identity, scale=lnr[:, par:par + 1],
                bias=lnb[:, par:par + 1])), f"bscAP{par}")
        for par in range(2):
            P.wait(sc, f"bvLS{par}")
            if par == 0:
                P.wait(sc, "bvWG1", dit=-1)
            else:
                P.wait(sc, "bvWG0")
            P.op(c_bsc, (lambda: sc.activation(
                rr["prob"][:, :], rr["Lg"][:, :], AF.Exp)), f"bscEX{par}")

    ph2.begin()
    a2_body(ph2)
    ph2.mid()
    for p in range(2):
        ots0[p][0] = c_ots[p].count
        xrs0[p][0] = c_xrs[p].count
        bx1f0[p][0] = c_bx1f[p].count
        bx1t0[p][0] = c_bx1t[p].count
        bcwt0[p][0] = c_bcwt[p].count
    gp.dma_start(WOs, woT.rearrange("(k p) e -> p k e", p=128)).then_inc(c_preg.sem, 16)
    c_preg.count += 16
    sy.wait_ge(c_aod[0].sem, c_aod[0].count)
    sy.wait_ge(c_aod[1].sem, c_aod[1].count)
    for pi in range(16):
        for par in range(2):
            tt = 2 * pi + par
            b = tt // 16
            t = tt % 16
            if tt >= 2:
                sy.wait_ge(c_bmm.sem, ph2.val(f"bmmWO{(par * 4) + 3}", pi - 1))
            sy.dma_start(bvv(o_OTs[par], 2048).rearrange("p (k t) -> p k t", k=NK),
                         ot_hbm[b].rearrange("(k p) t -> p k t", p=128)
                         [:, :, t * 128:(t + 1) * 128]).then_inc(c_ots[par].sem, 16)
            c_ots[par].count += 16
            if tt >= 2:
                sy.wait_ge(c_bv.sem, ph2.val(f"br1_{par * 4 + 3}", pi - 1))
            sy.dma_start(fvv(o_xres[par], 2048),
                         xin[tt * 128:(tt + 1) * 128, :]).then_inc(c_xrs[par].sem, 16)
            c_xrs[par].count += 16
        for par in range(2):
            tt = 2 * pi + par
            sy.wait_ge(c_bsc.sem, ph2.val(f"bscAP{par}", pi))
            sy.dma_start(x1f[tt * 128:(tt + 1) * 128, :],
                         fvv(o_x1[par], 2048)).then_inc(c_bx1f[par].sem, 16)
            c_bx1f[par].count += 16
            sy.wait_ge(c_bv.sem, ph2.val(f"bvTS{par}_1", pi))
            sy.dma_start(x1t.rearrange("(k p) t -> p k t", p=128)
                         [:, :, tt * 128:(tt + 1) * 128],
                         bvv(o_x1ts[par], 2048).rearrange("p (k t) -> p k t", k=NK)
                         ).then_inc(c_bx1t[par].sem, 16)
            c_bx1t[par].count += 16
            if tt >= 2:
                sy.wait_ge(c_bv.sem, ph2.val(f"bvCWT{par}", pi))
                sy.dma_start(cwTd[:, (tt - 2) * 128:(tt - 1) * 128],
                             fvv(o_cwt[par], 128)[0:32, :]
                             ).then_inc(c_bcwt[par].sem, 16)
                c_bcwt[par].count += 16
    te.wait_ge(c_preg.sem, c_preg.count)
    ph2.prepare(st)
    with nc.Fori(0, 16, engines=CENG) as pi2:
        ph2.i = pi2
        a2_body(ph2)
    ph2.finish(16)
    for par in range(2):
        te.wait_ge(c_bv.sem, ph2.val(f"bvCW{par}", 15))
        if par == 0:
            te.wait_ge(c_bv.sem, ph2.val("bvCWT1", 15))
        else:
            te.wait_ge(c_bv.sem, c_bv.count)
        te.transpose(pf[5][0:32, 0:128],
                     (rr["cw"] if par == 0 else rr["cw1"])[:, :],
                     idf[:, :]).then_inc(c_bmm.sem, 1)
        c_bmm.count += 1
        ve.wait_ge(c_bmm.sem, c_bmm.count)
        ve.wait_ge(c_bcwt[par].sem, c_bcwt[par].count)
        ve.tensor_copy(fvv(o_cwt[par], 128)[0:32, :],
                       pf[5][0:32, 0:128]).then_inc(c_bv.sem, 1)
        c_bv.count += 1
        sy.wait_ge(c_bv.sem, c_bv.count)
        sy.dma_start(cwTd[:, (30 + par) * 128:(31 + par) * 128],
                     fvv(o_cwt[par], 128)[0:32, :]
                     ).then_inc(c_bcwt[par].sem, 16)
        c_bcwt[par].count += 16
    nc.all_engine_barrier()

    if debug_phase == "A":
        for p in range(2):
            sy.wait_ge(c_bx1f[p].sem, c_bx1f[p].count)
            sy.wait_ge(c_bx1t[p].sem, c_bx1t[p].count)
            sy.wait_ge(c_bcwt[p].sem, c_bcwt[p].count)
        drain_all_dma()
        st.close()
        return nc

    ph3 = Phase(nc, {c_cmm: te, c_csc: sc, c_cv: ve})
    crow0 = [0]
    cx10 = [[0], [0]]
    cgd0 = [[0], [0]]
    chd0 = [[0], [0]]

    def c_body(P):
        for par in range(2):
            W = WGU[par]
            CWR = cwrow[:, 0:T]
            P.wait_ext(te, c_cgd[par], 32, 32, cgd0[par])
            P.wait_ext(te, c_crow, 32, (par + 1) * 16, crow0)
            for n in range(8):
                np_ = n % 2
                XC = X1C[np_]
                P.wait_ext(te, c_cx1[n % 2], 128,
                           par * 64 + (n // 2 + 1) * 16, cx10[n % 2])
                if n >= 2:
                    P.wait(te, f"cvH{par}_{n-2}_5")
                else:
                    P.wait(te, f"cvH{1-par}_{n+6}_5", dit=(-1 if par == 0 else 0))
                P.op(c_cmm, (lambda CWR=CWR, n=n, np_=np_: te.matmul(
                    pf[4 + np_][:, :], ones1[:, :], CWR[:, n * 512:(n + 1) * 512],
                    start=True, stop=True)), f"cmmB{par}_{n}")
                for mp in range(6):
                    gidx = n * 6 + mp
                    if gidx >= 2:
                        P.wait(te, f"cvT1{par}_{gidx-2}")
                    else:
                        P.wait(te, f"cvT1{1-par}_{gidx+46}",
                               dit=(-1 if par == 0 else 0))
                    for k in range(16):
                        fn = (lambda k=k, mp=mp, n=n, gidx=gidx, W=W, XC=XC:
                              te.matmul(pf[gidx % 2][:, :],
                                        W[:, k, mp * 128:(mp + 1) * 128],
                                        XC[:, k, :],
                                        start=(k == 0), stop=(k == 15)))
                        if k == 15:
                            P.op(c_cmm, fn, f"cmmG{par}_{gidx}")
                        else:
                            P.raw(fn)
                    for k in range(16):
                        fn = (lambda k=k, mp=mp, n=n, gidx=gidx, W=W, XC=XC:
                              te.matmul(pf[2 + gidx % 2][:, :],
                                        W[:, k, 768 + mp * 128:768 + (mp + 1) * 128],
                                        XC[:, k, :],
                                        start=(k == 0), stop=(k == 15)))
                        if k == 15:
                            P.op(c_cmm, fn, f"cmmU{par}_{gidx}")
                        else:
                            P.raw(fn)
            for n in range(8):
                for mp in range(6):
                    gidx = n * 6 + mp
                    P.wait(sc, f"cmmG{par}_{gidx}")
                    P.op(c_csc, (lambda gidx=gidx: sc.activation(
                        fvv(o_sg[gidx % 2], 512), pf[gidx % 2][:, :], AF.Silu)),
                        f"csg{par}_{gidx}")
            for n in range(8):
                np_ = n % 2
                for mp in range(6):
                    gidx = n * 6 + mp
                    P.wait(ve, f"cmmU{par}_{gidx}")
                    P.wait(ve, f"csg{par}_{gidx}")
                    P.op(c_cv, (lambda gidx=gidx: ve.tensor_tensor(
                        fvv(o_t1[gidx % 2], 512), pf[2 + gidx % 2][:, :],
                        fvv(o_sg[gidx % 2], 512), ALU.mult)),
                        f"cvT1{par}_{gidx}")
                    P.wait(ve, f"cmmB{par}_{n}")
                    if mp == 0:
                        P.wait_ext(ve, c_chd[n % 2], 128,
                                   par * 64 + (n // 2) * 16, chd0[n % 2])
                    P.wait_self(ve, c_cv)
                    P.op(c_cv, (lambda gidx=gidx, mp=mp, np_=np_: ve.tensor_tensor(
                        HST[np_][:, mp, :], fvv(o_t1[gidx % 2], 512),
                        pf[4 + np_][:, :], ALU.mult)), f"cvH{par}_{n}_{mp}")

    ph3.begin()
    c_body(ph3)
    ph3.mid()
    crow0[0] = c_crow.count
    for p in range(2):
        cx10[p][0] = c_cx1[p].count
        cgd0[p][0] = c_cgd[p].count
        chd0[p][0] = c_chd[p].count
    for p in range(2):
        for chx in (c_bx1f[p], c_bx1t[p], c_bcwt[p]):
            sy.wait_ge(chx.sem, chx.count)
            gp.wait_ge(chx.sem, chx.count)
            ve.wait_ge(chx.sem, chx.count)
            sc.wait_ge(chx.sem, chx.count)
    for e in range(NEXP):
        pi, par = divmod(e, 2)
        if e >= 2:
            gp.wait_ge(c_cmm.sem, ph3.val(f"cmmU{par}_47", pi - 1))
        gp.dma_start(WGU[par][:, :, 0:768],
                     wguTd[e].rearrange("(k p) m -> p k m", p=128)[:, :, 0:768]
                     ).then_inc(c_cgd[par].sem, 16)
        gp.dma_start(WGU[par][:, :, 768:1536],
                     wguTd[e].rearrange("(k p) m -> p k m", p=128)[:, :, 768:1536]
                     ).then_inc(c_cgd[par].sem, 16)
        c_cgd[par].count += 32
        for q in range(4):
            gp.dma_start(bvv(o_wdc[q % 2], 3072).rearrange("p (k m) -> p k m", k=6),
                         wdTd[e].rearrange("(k p) m -> p k m", p=128)
                         [:, :, q * 512:(q + 1) * 512]).then_inc(c_cwc.sem, 16)
            c_cwc.count += 16
            gp.wait_ge(c_cwc.sem, c_cwc.count)
            gp.dma_start(wd_b[e].rearrange("(k p) m -> p k m", p=128)
                         [:, :, q * 512:(q + 1) * 512],
                         bvv(o_wdc[q % 2], 3072).rearrange("p (k m) -> p k m", k=6)
                         ).then_inc(c_cwc.sem, 16)
            c_cwc.count += 16
            gp.wait_ge(c_cwc.sem, c_cwc.count)
        if e >= 1:
            sy.wait_ge(c_cmm.sem,
                       ph3.val(f"cmmB{1 - par}_7", pi if par == 1 else pi - 1))
        sy.dma_start(cwrow[:, 0:T],
                     cwTd[e:e + 1, :]).then_inc(c_crow.sem, 16)
        c_crow.count += 16
        def c_load(n):
            gni = e * 8 + n
            if gni >= 2:
                sy.wait_ge(c_cmm.sem,
                           ph3.val(f"cmmU{(gni - 2) % 16 // 8}_{((gni - 2) % 8) * 6 + 5}",
                                   (gni - 2) // 16))
            sy.dma_start(X1C[n % 2],
                         x1t.rearrange("(k p) t -> p k t", p=128)
                         [:, :, n * 512:(n + 1) * 512]).then_inc(c_cx1[n % 2].sem, 16)
            c_cx1[n % 2].count += 16

        def c_store(n):
            sy.wait_ge(c_cv.sem, ph3.val(f"cvH{par}_{n}_5", pi))
            sy.dma_start(hbuf[e].rearrange("(k p) t -> p k t", p=128)
                         [:, :, n * 512:(n + 1) * 512],
                         HST[n % 2]).then_inc(c_chd[n % 2].sem, 16)
            c_chd[n % 2].count += 16

        for n in range(8):
            c_load(n)
            if n >= 1:
                c_store(n - 1)
        c_store(7)
    ph3.prepare(st)
    with nc.Fori(0, 16, engines=CENG) as ci:
        ph3.i = ci
        c_body(ph3)
    ph3.finish(16)
    nc.all_engine_barrier()

    if debug_phase == "C":
        sy.wait_ge(c_chd[0].sem, c_chd[0].count)
        sy.wait_ge(c_chd[1].sem, c_chd[1].count)
        drain_all_dma()
        st.close()
        return nc

    chd_full = [c_chd[0].count, c_chd[1].count]
    cwc_full = c_cwc.count
    for s in range(NSW):
        ph4 = Phase(nc, {c_dmm: te, c_dv: ve})
        dh0 = [[0], [0]]
        dgd0 = [[0], [0]]

        def d_body(P, s=s):
            for par in range(2):
                WD = WDS[par]
                P.wait_ext(te, c_dgd[par], 16, 16, dgd0[par])
                for n in range(8):
                    np_ = n % 2
                    P.wait_ext(te, c_dh[n % 2], 128,
                               par * 64 + (n // 2 + 1) * 16, dh0[n % 2])
                    for m in range(4):
                        dg = n * 4 + m
                        bank = pf[dg % 4]
                        if dg >= 4:
                            P.wait(te, f"dv{par}_{dg-4}")
                        else:
                            P.wait(te, f"dv{1-par}_{dg+28}",
                                   dit=(-1 if par == 0 else 0))
                        for k in range(6):
                            fn = (lambda k=k, m=m, np_=np_, WD=WD, bank=bank:
                                  te.matmul(bank[:, :],
                                            WD[:, k, m * 128:(m + 1) * 128],
                                            HC[np_][:, k, :],
                                            start=(k == 0), stop=(k == 5)))
                            if k == 5:
                                P.op(c_dmm, fn, f"dmm{par}_{dg}")
                            else:
                                P.raw(fn)
                for n in range(8):
                    for m in range(4):
                        dg = n * 4 + m
                        bank = pf[dg % 4]
                        P.wait(ve, f"dmm{par}_{dg}")
                        P.op(c_dv, (lambda n=n, m=m, bank=bank: ve.tensor_tensor(
                            fvv(o_y[m], 4096)[:, n * 512:(n + 1) * 512],
                            fvv(o_y[m], 4096)[:, n * 512:(n + 1) * 512],
                            bank[:, :], ALU.add)), f"dv{par}_{dg}")

        for m in range(4):
            ve.memset(fvv(o_y[m], 4096), 0.0).then_inc(c_dv.sem, 1)
            c_dv.count += 1

        ph4.begin()
        d_body(ph4)
        ph4.mid()
        for p in range(2):
            dh0[p][0] = c_dh[p].count
            dgd0[p][0] = c_dgd[p].count
        gp.wait_ge(c_cwc.sem, cwc_full)
        sy.wait_ge(c_chd[0].sem, chd_full[0])
        sy.wait_ge(c_chd[1].sem, chd_full[1])
        for p in range(2):
            sy.wait_ge(c_bx1f[p].sem, c_bx1f[p].count)
        for e in range(NEXP):
            pi, par = divmod(e, 2)
            if e >= 2:
                gp.wait_ge(c_dmm.sem, ph4.val(f"dmm{par}_31", pi - 1))
            gp.dma_start(WDS[par],
                         wd_b[e].rearrange("(k p) m -> p k m", p=128)
                         [:, :, s * 512:(s + 1) * 512]).then_inc(c_dgd[par].sem, 16)
            c_dgd[par].count += 16
            for n in range(8):
                gni = e * 8 + n
                if gni >= 2:
                    sy.wait_ge(c_dmm.sem,
                               ph4.val(f"dmm{(gni - 2) % 16 // 8}_{((gni - 2) % 8) * 4 + 3}",
                                       (gni - 2) // 16))
                sy.dma_start(HC[n % 2],
                             hbuf[e].rearrange("(k p) t -> p k t", p=128)
                             [:, :, n * 512:(n + 1) * 512]).then_inc(c_dh[n % 2].sem, 16)
                c_dh[n % 2].count += 16
        ph4.prepare(st)
        with nc.Fori(0, 16, engines=CENG) as di:
            ph4.i = di
            d_body(ph4)
        ph4.finish(16)

        dmm_full = c_dmm.count
        tail_v0 = c_dv.count
        tail_t0 = c_dmm.count
        dxf_b = [c_dxf[0].count, c_dxf[1].count]
        drd_b = [c_drd[0].count, c_drd[1].count]
        for ttl in range(NT):
            par = ttl % 2
            bank = pf[4] if par == 0 else pf[5]
            if ttl >= 2:
                sy.wait_ge(c_dv.sem, tail_v0 + (ttl - 2) * 6 + 2)
            sy.dma_start(fvv(o_x1ft[par], 512),
                         x1f[ttl * 128:(ttl + 1) * 128, s * 512:(s + 1) * 512]
                         ).then_inc(c_dxf[par].sem, 16)
            c_dxf[par].count += 16
            if ttl == 0:
                te.wait_ge(c_dv.sem, tail_v0)
            if ttl >= 2:
                te.wait_ge(c_dv.sem, tail_v0 + (ttl - 2) * 6 + 2)
            for m in range(4):
                te.transpose(bank[:, m * 128:(m + 1) * 128],
                             fvv(o_y[m], 4096)[:, ttl * 128:(ttl + 1) * 128],
                             idf[:, :]).then_inc(c_dmm.sem, 1)
                c_dmm.count += 1
            ve.wait_ge(c_dmm.sem, tail_t0 + (ttl + 1) * 4)
            ve.wait_ge(c_dxf[par].sem, c_dxf[par].count)
            ve.wait_ge(c_drd[par].sem, c_drd[par].count)
            ve.tensor_tensor(fvv(o_rst[par], 512), bank[:, :],
                             fvv(o_x1ft[par], 512), ALU.add).then_inc(c_dv.sem, 1)
            c_dv.count += 1
            ve.wait_ge(c_dv.sem, c_dv.count)
            ve.reduce_sum(c1[:, :], fvv(o_rst[par], 512), AX.X).then_inc(c_dv.sem, 1)
            c_dv.count += 1
            ve.wait_ge(c_dv.sem, c_dv.count)
            ve.tensor_tensor(statm[:, ttl:ttl + 1], statm[:, ttl:ttl + 1],
                             c1[:, :], ALU.add).then_inc(c_dv.sem, 1)
            c_dv.count += 1
            ve.tensor_tensor(fvv(o_sqd[par], 512), fvv(o_rst[par], 512),
                             fvv(o_rst[par], 512), ALU.mult).then_inc(c_dv.sem, 1)
            c_dv.count += 1
            ve.wait_ge(c_dv.sem, c_dv.count)
            ve.reduce_sum(c2[:, :], fvv(o_sqd[par], 512), AX.X).then_inc(c_dv.sem, 1)
            c_dv.count += 1
            ve.wait_ge(c_dv.sem, c_dv.count)
            ve.tensor_tensor(stats[:, ttl:ttl + 1], stats[:, ttl:ttl + 1],
                             c2[:, :], ALU.add).then_inc(c_dv.sem, 1)
            c_dv.count += 1
            sy.wait_ge(c_dv.sem, tail_v0 + ttl * 6 + 1)
            sy.dma_start(rres[ttl * 128:(ttl + 1) * 128, s * 512:(s + 1) * 512],
                         fvv(o_rst[par], 512)).then_inc(c_drd[par].sem, 16)
            c_drd[par].count += 16
        nc.all_engine_barrier()

    ed0 = [c_ed[0].count, c_ed[1].count]
    ev0 = c_ev.count
    es0 = c_esc.count
    eo0 = [c_out[0].count, c_out[1].count]
    sy.wait_ge(c_drd[0].sem, c_drd[0].count)
    sy.wait_ge(c_drd[1].sem, c_drd[1].count)
    for ttl in range(NT):
        par = ttl % 2
        if ttl >= 2:
            sy.wait_ge(c_esc.sem, es0 + (ttl - 2) * 3 + 3)
        sy.dma_start(fvv(o_rrE[par], 2048),
                     rres[ttl * 128:(ttl + 1) * 128, :]).then_inc(c_ed[par].sem, 16)
        c_ed[par].count += 16
        ve.tensor_scalar(lnm[:, par:par + 1], statm[:, ttl:ttl + 1], 1.0 / D,
                         None, ALU.mult).then_inc(c_ev.sem, 1)
        c_ev.count += 1
        ve.tensor_scalar(c1[:, :], stats[:, ttl:ttl + 1], 1.0 / D,
                         None, ALU.mult).then_inc(c_ev.sem, 1)
        c_ev.count += 1
        ve.wait_ge(c_ev.sem, c_ev.count)
        ve.tensor_tensor(c2[:, :], lnm[:, par:par + 1], lnm[:, par:par + 1],
                         ALU.mult).then_inc(c_ev.sem, 1)
        c_ev.count += 1
        ve.wait_ge(c_ev.sem, c_ev.count)
        ve.tensor_tensor(lnv[:, par:par + 1], c1[:, :], c2[:, :],
                         ALU.subtract).then_inc(c_ev.sem, 1)
        c_ev.count += 1
        sc.wait_ge(c_ev.sem, c_ev.count)
        sc.activation(lnr[:, par:par + 1], lnv[:, par:par + 1], AF.Ln,
                      bias=cst[:, C_EPS_LN:C_EPS_LN + 1]).then_inc(c_esc.sem, 1)
        c_esc.count += 1
        sc.wait_ge(c_esc.sem, c_esc.count)
        sc.activation(lnr[:, par:par + 1], lnr[:, par:par + 1], AF.Exp,
                      scale=cst[:, C_NEGH:C_NEGH + 1]).then_inc(c_esc.sem, 1)
        c_esc.count += 1
        ve.wait_ge(c_esc.sem, c_esc.count)
        ve.tensor_scalar(lnb[:, par:par + 1], lnm[:, par:par + 1],
                         lnr[:, par:par + 1], -1.0, ALU.mult, ALU.mult
                         ).then_inc(c_ev.sem, 1)
        c_ev.count += 1
        sc.wait_ge(c_ev.sem, c_ev.count)
        sc.wait_ge(c_esc.sem, c_esc.count)
        sc.wait_ge(c_ed[par].sem, c_ed[par].count)
        if ttl >= 2:
            sc.wait_ge(c_out[par].sem, c_out[par].count)
        sc.activation(fvv(o_oe[par], 2048), fvv(o_rrE[par], 2048), AF.Identity,
                      scale=lnr[:, par:par + 1], bias=lnb[:, par:par + 1]
                      ).then_inc(c_esc.sem, 1)
        c_esc.count += 1
        sy.wait_ge(c_esc.sem, c_esc.count)
        sy.dma_start(out[ttl * 128:(ttl + 1) * 128, :],
                     fvv(o_oe[par], 2048)).then_inc(c_out[par].sem, 16)
        c_out[par].count += 16
    sy.wait_ge(c_out[0].sem, c_out[0].count)
    sy.wait_ge(c_out[1].sem, c_out[1].count)
    drain_all_dma()
    st.close()
    return nc


_NC_CACHE = {}


def kernel(L0_attn_x, L0_attn_wq, L0_attn_wk, L0_attn_wv, L0_attn_wu,
           L0_attn_wo, L0_ffn_gate_weight, L0_ffn_gate_up_proj,
           L0_ffn_down_proj, L0_ffn_similarity):
    x = np.ascontiguousarray(np.asarray(L0_attn_x, dtype=np.float32))
    wq = np.asarray(L0_attn_wq, dtype=np.float32)
    wk = np.asarray(L0_attn_wk, dtype=np.float32)
    wv = np.asarray(L0_attn_wv, dtype=np.float32)
    wu = np.asarray(L0_attn_wu, dtype=np.float32)
    wo = np.asarray(L0_attn_wo, dtype=np.float32)
    gate = np.asarray(L0_ffn_gate_weight, dtype=np.float32)
    gup = np.asarray(L0_ffn_gate_up_proj, dtype=np.float32)
    dwn = np.asarray(L0_ffn_down_proj, dtype=np.float32)
    sim = np.ascontiguousarray(np.asarray(L0_ffn_similarity, dtype=np.float32))

    wqT = np.ascontiguousarray(wq.T)
    wkT = np.ascontiguousarray(wk.T)
    wvT = np.ascontiguousarray(wv.T)
    wuT = np.ascontiguousarray(wu.T)
    woT = np.ascontiguousarray(wo.T)
    gateT = np.ascontiguousarray(gate.T)
    wguT = np.ascontiguousarray(gup.transpose(0, 2, 1))
    wdT = np.ascontiguousarray(dwn.transpose(0, 2, 1))

    consts = np.zeros((128, NCONST), dtype=np.float32)
    consts[:, C_ISCALE] = ISCALE
    consts[:, C_SRMS_EPS] = 1e-20
    consts[:, C_NEGH] = -0.5
    consts[:, C_EPS_LN] = 1e-5
    consts[:, C_SRMS_LNC] = 0.5 * math.log(128.0)
    identb = np.eye(128, dtype=np.float32)
    identf = np.eye(128, dtype=np.float32)
    onesf = np.ones((1, 128), dtype=np.float32)
    i32 = np.eye(32, dtype=np.float32)

    key = DEBUG_PHASE
    if key not in _NC_CACHE:
        _NC_CACHE[key] = build_nc(key)
    nc = _NC_CACHE[key]

    xr = x.reshape(8, T, D)
    in_maps = []
    for c in range(8):
        in_maps.append({
            "xin": xr[c], "wqT": wqT, "wkT": wkT, "wvT": wvT, "wuT": wuT,
            "woT": woT, "gateTd": gateT, "wguTd": wguT, "wdTd": wdT,
            "simin": sim, "i32in": i32, "constsin": consts,
            "identbin": identb, "identfin": identf, "onesfin": onesf,
        })
    res = run_bass_kernel_spmd(nc, in_maps, core_ids=list(range(8)))
    outs = np.stack([res.results[c]["out"] for c in range(8)])
    return outs.reshape(16, 2048, 2048)


# version 06
# speedup vs baseline: 1.0450x; 1.0450x over previous
import math
import contextlib
import numpy as np
import concourse.bass as bass
import concourse.mybir as mybir
from concourse.bass_utils import run_bass_kernel_spmd

F32 = mybir.dt.float32
BF16 = mybir.dt.bfloat16
AF = mybir.ActivationFunctionType
ALU = mybir.AluOpType
AX = mybir.AxisListType
ET = mybir.EngineType

D = 2048
NB = 2
T = NB * 2048
NK = 16
NT = 32
NEXP = 32
DFF = 768
GU = 1536
ISCALE = 1.0 / math.sqrt(128.0)
NSW = 4
SWD = 512

C_ISCALE, C_SRMS_EPS, C_NEGH, C_EPS_LN, C_SRMS_LNC = 0, 1, 2, 3, 4
NCONST = 8
CENG = [ET.PE, ET.DVE, ET.Activation]

DEBUG_PHASE = "ALL"


class Chan:
    def __init__(self, st, nc, name, step=1):
        self.sem = st.enter_context(nc.semaphore(name))
        self.step = step
        self.count = 0
        self.name = name


_PHASE_SEQ = [0]


class Phase:
    def __init__(self, nc, chan_eng):
        self.nc = nc
        self.chan_eng = dict(chan_eng)
        self.chans = list(chan_eng)
        self.dry = True
        self.base = {}
        self.per = {}
        self.labels = {}
        self.i = None
        self.wrec = []
        self.wseq = {}
        self.wcur = {}
        self.wreg = {}
        self.winfo = {}

    def begin(self):
        self.dry = True
        self.base = {c: c.count for c in self.chans}
        self.labels = {}
        self.wrec = []

    def op(self, ch, fn, label=None):
        ch.count += ch.step
        if label is not None:
            self.labels[label] = (ch, ch.count - self.base[ch])
        if not self.dry:
            fn().then_inc(ch.sem, ch.step)

    def raw(self, fn):
        if not self.dry:
            fn()

    def _wkey(self, eng, ch):
        return (eng.engine, ch.name)

    def wait(self, eng, label, dit=0):
        if self.dry:
            self.wrec.append((eng, None, "lbl", (label, dit)))
        else:
            self._emit_wait(eng)

    def wait_self(self, eng, ch):
        if self.dry:
            rel = ch.count - self.base[ch]
            self.wrec.append((eng, ch, "self", rel))
        else:
            self._emit_wait(eng)

    def wait_ext(self, eng, ch, per, off, base_list):
        if self.dry:
            self.wrec.append((eng, ch, "ext", (per, off, base_list)))
        else:
            self._emit_wait(eng)

    def _emit_wait(self, eng):
        key = eng.engine
        seq = self.wemit[key]
        cur = self.wcurs[key]
        self.wcurs[key] = cur + 1
        item = seq[cur]
        if item is None:
            return
        ch, reg, delta = item
        if delta != 0:
            eng.reg_add(reg, reg, delta)
        eng.wait_ge(ch.sem, reg)

    def mid(self):
        self.per = {c: c.count - self.base[c] for c in self.chans}
        for c in self.chans:
            c.count = self.base[c]
        negoffs = {c: set() for c in self.chans}
        for (eng, ch, kind, payload) in self.wrec:
            if kind == "lbl":
                label, dit = payload
                lch, idx = self.labels[label]
                if lch in negoffs:
                    off = dit * self.per[lch] + idx
                    if off <= 0:
                        negoffs[lch].add(off)
            elif kind == "self":
                pass
        for c in self.chans:
            if self.per[c] <= 0:
                continue
            pre = c.count
            stops = sorted(pre + self.per[c] + o for o in negoffs[c]
                           if pre + self.per[c] + o > pre)
            for v in stops:
                if v > c.count:
                    self.chan_eng[c].sem_inc(c.sem, v - c.count)
                    c.count = v
            final = pre + self.per[c]
            if final > c.count:
                self.chan_eng[c].sem_inc(c.sem, final - c.count)
                c.count = final
        self.base = {c: c.count for c in self.chans}
        self.dry = False

    def prepare(self, st):
        self._owners = []
        resolved = []
        for (eng, ch, kind, payload) in self.wrec:
            if kind == "lbl":
                label, dit = payload
                lch, idx = self.labels[label]
                resolved.append((eng, lch, self.per[lch], self.base[lch],
                                 dit * self.per[lch] + idx))
            elif kind == "self":
                rel = payload
                resolved.append((eng, ch, self.per[ch], self.base[ch], rel))
            else:
                per, off, base_list = payload
                resolved.append((eng, ch, per, base_list[0], off))
        lastoff = {}
        o_max = {}
        per_engine = {}
        for (eng, ch, per, base, off) in resolved:
            k = (eng.engine, ch.name)
            o_max[k] = max(o_max.get(k, -10 ** 9), off)
        self.wemit = {}
        self.wcurs = {}
        regs = {}
        for (eng, ch, per, base, off) in resolved:
            k = (eng.engine, ch.name)
            ek = eng.engine
            self.wemit.setdefault(ek, [])
            self.wcurs[ek] = 0
            if off <= o_max[k] - per:
                self.wemit[ek].append(None)
                continue
            prev = lastoff.get(k)
            if prev is not None and off <= prev:
                self.wemit[ek].append(None)
                continue
            if k not in regs:
                _PHASE_SEQ[0] += 1
                reg = eng.alloc_register(f"w{_PHASE_SEQ[0]}_{ch.name}")
                self._owners.append((eng, reg))
                init = base + o_max[k] - per
                assert init >= 0, f"reg init negative for {k}: {init}"
                eng.reg_mov(reg, init)
                regs[k] = reg
                delta = per + off - o_max[k]
            else:
                reg = regs[k]
                delta = off - lastoff[k]
            assert delta > 0
            self.wemit[ek].append((ch, reg, delta))
            lastoff[k] = off

    def finish(self, n_iters):
        for c in self.chans:
            c.count = self.base[c] + n_iters * self.per[c]
        for eng, reg in self._owners:
            eng.free_register(reg)
        self._owners = []

    def val(self, label, it):
        ch, idx = self.labels[label]
        return max(0, self.base[ch] + it * self.per[ch] + idx)


def build_nc(debug_phase="ALL"):
    nc = bass.Bass("TRN2", target_bir_lowering=False)
    st = contextlib.ExitStack()
    st.__enter__()

    EIn = "ExternalInput"
    xin = nc.dram_tensor("xin", [T, D], F32, kind=EIn).ap()
    wqT = nc.dram_tensor("wqT", [D, D], F32, kind=EIn).ap()
    wkT = nc.dram_tensor("wkT", [D, D], F32, kind=EIn).ap()
    wvT = nc.dram_tensor("wvT", [D, D], F32, kind=EIn).ap()
    wuT = nc.dram_tensor("wuT", [D, D], F32, kind=EIn).ap()
    woT = nc.dram_tensor("woT", [D, D], F32, kind=EIn).ap()
    gateTd = nc.dram_tensor("gateTd", [D, NEXP], F32, kind=EIn).ap()
    wguTd = nc.dram_tensor("wguTd", [NEXP, D, GU], F32, kind=EIn).ap()
    wdTd = nc.dram_tensor("wdTd", [NEXP, DFF, D], F32, kind=EIn).ap()
    simin = nc.dram_tensor("simin", [32, 32], F32, kind=EIn).ap()
    i32in = nc.dram_tensor("i32in", [32, 32], F32, kind=EIn).ap()
    constsin = nc.dram_tensor("constsin", [128, NCONST], F32, kind=EIn).ap()
    identbin = nc.dram_tensor("identbin", [128, 128], F32, kind=EIn).ap()
    identfin = nc.dram_tensor("identfin", [128, 128], F32, kind=EIn).ap()
    onesfin = nc.dram_tensor("onesfin", [1, 128], F32, kind=EIn).ap()

    out = nc.dram_tensor("out", [T, D], F32, kind="ExternalOutput").ap()

    dbgA = debug_phase == "A"
    dbgC = debug_phase == "C"
    ot_hbm = nc.dram_tensor("ot_hbm", [NB, D, 2048], BF16, kind="Internal").ap()
    x1f = nc.dram_tensor("x1f", [T, D], F32,
                         kind="ExternalOutput" if dbgA else "Internal").ap()
    x1t = nc.dram_tensor("x1t", [D, T], BF16, kind="Internal").ap()
    cwTd = nc.dram_tensor("cwTd", [NEXP, T], F32,
                          kind="ExternalOutput" if dbgA else "Internal").ap()
    simfl = nc.dram_tensor("simfl", [32, 32], F32, kind="Internal").ap()
    simflT = nc.dram_tensor("simflT", [32, 32], F32, kind="Internal").ap()
    sink = nc.dram_tensor("sink", [16, 16], F32, kind="Internal").ap()
    wd_b = nc.dram_tensor("wd_b", [NEXP, DFF, D], BF16, kind="Internal").ap()
    hbuf = nc.dram_tensor("hbuf", [NEXP, DFF, T], BF16,
                          kind="ExternalOutput" if dbgC else "Internal").ap()
    rres = nc.dram_tensor("rres", [T, D], F32, kind="Internal").ap()

    def sbt(name, shape, dt):
        return st.enter_context(nc.sbuf_tensor(name, shape, dt)).ap()

    cst = sbt("cst", [128, NCONST], F32)
    idb = sbt("idb", [128, 128], BF16)
    idf = sbt("idf", [128, 128], F32)
    ones1 = sbt("ones1", [1, 128], F32)
    gateb = sbt("gateb", [128, NK * NEXP], BF16)
    sim32 = sbt("sim32", [32, 32], F32)
    simt1 = sbt("simt1", [32, 32], F32)
    i32s = sbt("i32s", [32, 32], F32)
    simrow = sbt("simrow", [1, 1024], F32)
    simB = sbt("simB", [128, 1024], F32)
    simBE = sbt("simBE", [128, 1024], F32)
    simt2 = sbt("simt2", [32, 32], F32)
    candb = sbt("candb", [128, 1024], F32)
    cwrow = sbt("cwrow", [1, T], F32)
    statm = sbt("statm", [128, NT], F32)
    stats = sbt("stats", [128, NT], F32)
    rr = {}
    for nmr in ("Lg", "prob", "rnk", "gt", "top8", "prim", "sec", "wgt",
                "C2", "keep", "acc", "t3", "t5", "cw", "cw1", "Mx"):
        rr[nmr] = sbt("r_" + nmr, [128, 32], F32)
    cwP = [rr["cw"], rr["cw1"]]
    c1 = sbt("c1", [128, 1], F32)
    c2 = sbt("c2", [128, 1], F32)
    ssb = sbt("ssb", [128, 32], F32)
    ssc = sbt("ssc", [128, 32], F32)
    lnm = sbt("lnm", [128, 2], F32)
    lnv = sbt("lnv", [128, 2], F32)
    lnr = sbt("lnr", [128, 2], F32)
    lnb = sbt("lnb", [128, 2], F32)

    ARN = 43616
    ARf = st.enter_context(nc.sbuf_tensor("AR", [128, ARN], F32)).ap()
    ARb = ARf.bitcast(BF16)

    def fvv(off, n):
        return ARf[:, off:off + n]

    def bvv(off, nb):
        return ARb[:, 2 * off:2 * off + nb]

    class Ar:
        def __init__(self, start=0):
            self.top = start
        def alloc(self, n):
            o = (self.top + 15) // 16 * 16
            self.top = o + n
            assert self.top <= ARN, f"arena overflow {self.top}"
            return o

    arA = Ar()
    o_xTb = arA.alloc(16384)
    o_KQ = arA.alloc(4096)
    o_VU = arA.alloc(4096)
    o_QT = arA.alloc(4096)
    o_sl = [arA.alloc(4096) for _ in range(2)]
    o_KV = arA.alloc(256)
    o_Ob = arA.alloc(2048)
    o_Of = [arA.alloc(256) for _ in range(2)]
    o_osq = [arA.alloc(512) for _ in range(2)]
    o_OTst = [arA.alloc(1024) for _ in range(2)]
    o_xtm = [o_KQ, o_KQ + 1024]

    arB = Ar()
    o_wo = arB.alloc(16384)
    o_OTs = [arB.alloc(1024) for _ in range(2)]
    o_xres = [arB.alloc(2048) for _ in range(2)]
    o_r1 = [arB.alloc(2048) for _ in range(2)]
    o_sq = [arB.alloc(2048) for _ in range(2)]
    o_x1 = [arB.alloc(2048) for _ in range(2)]
    o_x1b = [arB.alloc(1024) for _ in range(2)]
    o_x1ts = [arB.alloc(1024) for _ in range(2)]
    o_cwt = [arB.alloc(128) for _ in range(2)]

    arC = Ar()
    o_wgu = arC.alloc(2 * 12288)
    o_x1c = arC.alloc(2 * 4096)
    o_hst = [arC.alloc(1536) for _ in range(2)]
    o_sg = [arC.alloc(512) for _ in range(2)]
    o_t1 = [arC.alloc(512) for _ in range(2)]
    o_wdc = [arC.alloc(1536) for _ in range(2)]

    arD = Ar()
    o_y = [arD.alloc(4096) for _ in range(4)]
    o_hc = arD.alloc(2 * 1536)
    o_wds = arD.alloc(2 * 1536)
    o_x1ft = [arD.alloc(512) for _ in range(2)]
    o_rstD = arD.alloc(32 * 512)
    o_sqd = [arD.alloc(512) for _ in range(2)]

    arE = Ar()
    o_rrE = [arE.alloc(2048) for _ in range(2)]
    o_oe = [arE.alloc(2048) for _ in range(2)]

    pf = [st.enter_context(nc.psum_tensor(f"pf{i}", [128, 512], F32)).ap()
          for i in range(6)]
    pbt = st.enter_context(nc.psum_tensor("pbt", [128, 1024], BF16)).ap()

    ALL_CH = []

    def CH(name, step=1):
        c = Chan(st, nc, name, step)
        ALL_CH.append(c)
        return c

    c_pre = CH("c_pre", 16)
    c_preg = CH("c_preg", 16)
    c_prev = CH("c_prev")
    c_xd = [CH("c_xd0", 16), CH("c_xd1", 16)]
    c_a0t = CH("c_a0t")
    c_a0v = CH("c_a0v")
    c_slK = CH("c_slK", 16)
    c_slV = CH("c_slV", 16)
    c_slQ = CH("c_slQ", 16)
    c_slU = CH("c_slU", 16)
    c_amm = CH("c_amm")
    c_asc = CH("c_asc")
    c_avv = CH("c_avv")
    c_aod = [CH("c_aod0", 16), CH("c_aod1", 16)]
    c_ots = [CH("c_ots0", 16), CH("c_ots1", 16)]
    c_xrs = [CH("c_xrs0", 16), CH("c_xrs1", 16)]
    c_bmm = CH("c_bmm")
    c_bv = CH("c_bv")
    c_bsc = CH("c_bsc")
    c_bx1f = [CH("c_bx1f0", 16), CH("c_bx1f1", 16)]
    c_bx1t = [CH("c_bx1t0", 16), CH("c_bx1t1", 16)]
    c_bcwt = [CH("c_bcwt0", 16), CH("c_bcwt1", 16)]
    c_crow = CH("c_crow", 16)
    c_cx1 = [CH("c_cx10", 16), CH("c_cx11", 16)]
    c_cgd = [CH("c_cgd0", 16), CH("c_cgd1", 16)]
    c_cmm = CH("c_cmm")
    c_csc = CH("c_csc")
    c_cv = CH("c_cv")
    c_chd = [CH("c_chd0", 16), CH("c_chd1", 16)]
    c_cwc = CH("c_cwc", 16)
    c_dh = [CH("c_dh0", 16), CH("c_dh1", 16)]
    c_dxf = [CH("c_dxf0", 16), CH("c_dxf1", 16)]
    c_dgd = [CH("c_dgd0", 16), CH("c_dgd1", 16)]
    c_dmm = CH("c_dmm")
    c_dv = CH("c_dv")
    c_drd = [CH("c_drd0", 16), CH("c_drd1", 16)]
    c_ed = [CH("c_ed0", 16), CH("c_ed1", 16)]
    c_ev = CH("c_ev")
    c_esc = CH("c_esc")
    c_out = [CH("c_out0", 16), CH("c_out1", 16)]

    te, ve, sc, sy, gp = nc.tensor, nc.vector, nc.scalar, nc.sync, nc.gpsimd

    def drain_all_dma():
        for chd_ in ALL_CH:
            if chd_.step == 16 and chd_.count > 0:
                sy.wait_ge(chd_.sem, chd_.count)

    xTb = bvv(o_xTb, 32768).rearrange("p (k t) -> p k t", k=NK)
    KQ = bvv(o_KQ, 8192).rearrange("p (t e) -> p t e", t=16)
    VU = bvv(o_VU, 8192).rearrange("p (t e) -> p t e", t=16)
    QT = bvv(o_QT, 8192).rearrange("p (h t) -> p h t", h=4)
    SL0 = bvv(o_sl[0], 8192).rearrange("p (k e) -> p k e", k=NK)
    SL1 = bvv(o_sl[1], 8192).rearrange("p (k e) -> p k e", k=NK)
    KVb = bvv(o_KV, 512)
    Obb2 = [bvv(o_Ob, 4096)[:, 0:2048], bvv(o_Ob, 4096)[:, 2048:4096]]
    WOs = bvv(o_wo, 32768).rearrange("p (k e) -> p k e", k=NK)
    GTB = gateb.rearrange("p (k e) -> p k e", k=NK)
    WGU = [bvv(o_wgu + p * 12288, 24576).rearrange("p (k m) -> p k m", k=NK)
           for p in range(2)]
    X1C = [bvv(o_x1c + p * 4096, 8192).rearrange("p (k n) -> p k n", k=NK)
           for p in range(2)]
    HST = [bvv(o_hst[p], 3072).rearrange("p (m n) -> p m n", m=6)
           for p in range(2)]
    HC = [bvv(o_hc + p * 1536, 3072).rearrange("p (k n) -> p k n", k=6)
          for p in range(2)]
    WDS = [bvv(o_wds + p * 1536, 3072).rearrange("p (k m) -> p k m", k=6)
           for p in range(2)]

    sy.dma_start(cst[:, :], constsin).then_inc(c_pre.sem, 16)
    gp.dma_start(idb[:, :], identbin).then_inc(c_preg.sem, 16)
    sy.dma_start(idf[:, :], identfin).then_inc(c_pre.sem, 16)
    sy.dma_start(ones1[:, :], onesfin).then_inc(c_pre.sem, 16)
    sy.dma_start(sim32[:, :], simin).then_inc(c_pre.sem, 16)
    sy.dma_start(i32s[:, :], i32in).then_inc(c_pre.sem, 16)
    gp.dma_start(gateb[:, :].rearrange("p (k e) -> p k e", k=NK),
                 gateTd.rearrange("(k p) e -> p k e", p=128)).then_inc(c_preg.sem, 16)
    c_preg.count = 32
    c_pre.count = 80
    ve.wait_ge(c_pre.sem, 80)
    ve.tensor_tensor(simt1[:, :], sim32[:, :], i32s[:, :], ALU.mult).then_inc(c_prev.sem, 1)
    ve.wait_ge(c_prev.sem, 1)
    ve.tensor_tensor(simt1[:, :], sim32[:, :], simt1[:, :], ALU.subtract).then_inc(c_prev.sem, 1)
    ve.wait_ge(c_prev.sem, 2)
    ve.tensor_tensor(simt1[:, :], simt1[:, :], i32s[:, :], ALU.add).then_inc(c_prev.sem, 1)
    ve.memset(statm[:, :], 0.0).then_inc(c_prev.sem, 1)
    ve.memset(stats[:, :], 0.0).then_inc(c_prev.sem, 1)
    ve.memset(rr["cw"][:, :], 0.0).then_inc(c_prev.sem, 1)
    ve.memset(rr["cw1"][:, :], 0.0).then_inc(c_prev.sem, 1)
    c_prev.count = 7
    sy.wait_ge(c_prev.sem, 3)
    sy.dma_start(simfl, simt1[:, :]).then_inc(c_pre.sem, 16)
    c_pre.count += 16
    sy.wait_ge(c_pre.sem, c_pre.count)
    sy.dma_start(simrow[:, :],
                 simfl.rearrange("s e -> (s e)").rearrange("(o q) -> o q", o=1)
                 ).then_inc(c_pre.sem, 16)
    c_pre.count += 16
    te.wait_ge(c_prev.sem, 3)
    te.wait_ge(c_pre.sem, 80)
    te.transpose(pf[5][0:32, 0:32], simt1[:, :], idf[0:32, 0:32]).then_inc(c_amm.sem, 1)
    c_amm.count = 1
    ve.wait_ge(c_amm.sem, 1)
    ve.tensor_copy(simt2[:, :], pf[5][0:32, 0:32]).then_inc(c_prev.sem, 1)
    c_prev.count = 8
    sy.wait_ge(c_prev.sem, 8)
    sy.dma_start(simflT, simt2[:, :]).then_inc(c_pre.sem, 16)
    c_pre.count += 16
    te.wait_ge(c_pre.sem, c_pre.count)
    te.matmul(pf[0][:, :], ones1[:, :], simrow[:, 0:512],
              start=True, stop=True).then_inc(c_amm.sem, 1)
    te.matmul(pf[1][:, :], ones1[:, :], simrow[:, 512:1024],
              start=True, stop=True).then_inc(c_amm.sem, 1)
    c_amm.count = 3
    ve.wait_ge(c_amm.sem, 3)
    ve.tensor_copy(simB[:, 0:512], pf[0][:, :]).then_inc(c_prev.sem, 1)
    ve.tensor_copy(simB[:, 512:1024], pf[1][:, :]).then_inc(c_prev.sem, 1)
    c_prev.count = 10
    sy.wait_ge(c_amm.sem, 3)
    sy.dma_start(simrow[:, :],
                 simflT.rearrange("s e -> (s e)").rearrange("(o q) -> o q", o=1)
                 ).then_inc(c_pre.sem, 16)
    c_pre.count += 16
    te.wait_ge(c_pre.sem, c_pre.count)
    te.matmul(pf[2][:, :], ones1[:, :], simrow[:, 0:512],
              start=True, stop=True).then_inc(c_amm.sem, 1)
    te.matmul(pf[3][:, :], ones1[:, :], simrow[:, 512:1024],
              start=True, stop=True).then_inc(c_amm.sem, 1)
    c_amm.count = 5
    ve.wait_ge(c_amm.sem, 5)
    ve.tensor_copy(simBE[:, 0:512], pf[2][:, :]).then_inc(c_prev.sem, 1)
    ve.tensor_copy(simBE[:, 512:1024], pf[3][:, :]).then_inc(c_prev.sem, 1)
    c_prev.count = 12
    for chi, chp in enumerate((c_aod[0], c_aod[1], c_bx1f[0], c_bx1f[1],
                               c_bx1t[0], c_bx1t[1], c_bcwt[0], c_bcwt[1],
                               c_bcwt[0], c_bcwt[1], c_chd[0], c_chd[1])):
        if chp.count:
            sy.wait_ge(chp.sem, chp.count)
        sy.dma_start(sink[chi:chi + 1, 0:8], cst[0:1, 0:8]).then_inc(chp.sem, 16)
        chp.count += 16
    nc.all_engine_barrier()
    if debug_phase == "PRE":
        sy.wait_ge(c_pre.sem, c_pre.count)
        drain_all_dma()
        st.close()
        return nc

    for b in range(NB if debug_phase not in ("A0", "A1") else 1):
        a0_xd = [c_xd[0].count, c_xd[1].count]
        a0_t, a0_v = c_a0t.count, c_a0v.count
        for i in range(16):
            tt = b * 16 + i
            if i >= 2:
                gp.wait_ge(c_a0t.sem, a0_t + (i - 1) * 16)
            gp.dma_start(bvv(o_xtm[i % 2], 2048),
                         xin[tt * 128:(tt + 1) * 128, :]).then_inc(c_xd[i % 2].sem, 16)
            c_xd[i % 2].count += 16
        te.wait_ge(c_preg.sem, 32)
        for i in range(16):
            te.wait_ge(c_xd[i % 2].sem, a0_xd[i % 2] + (i // 2 + 1) * 16)
            if i >= 1:
                te.wait_ge(c_a0v.sem, a0_v + i * 2)
            for k in range(16):
                if k == 8:
                    te.wait_ge(c_a0v.sem, a0_v + i * 2 + 1)
                te.transpose(pbt[:, (k % 8) * 128:(k % 8) * 128 + 128],
                             bvv(o_xtm[i % 2], 2048)[:, k * 128:(k + 1) * 128],
                             idb[:, :]).then_inc(c_a0t.sem, 1)
                c_a0t.count += 1
            ve.wait_ge(c_a0t.sem, a0_t + i * 16 + 8)
            ve.tensor_copy(xTb[:, 0:8, i * 128:(i + 1) * 128],
                           pbt[:, :].rearrange("p (h t) -> p h t", h=8)
                           ).then_inc(c_a0v.sem, 1)
            c_a0v.count += 1
            ve.wait_ge(c_a0t.sem, a0_t + i * 16 + 16)
            ve.tensor_copy(xTb[:, 8:16, i * 128:(i + 1) * 128],
                           pbt[:, :].rearrange("p (h t) -> p h t", h=8)
                           ).then_inc(c_a0v.sem, 1)
            c_a0v.count += 1
        nc.all_engine_barrier()
        if debug_phase == "A0":
            drain_all_dma()
            st.close()
            return nc

        ph = Phase(nc, {c_amm: te, c_asc: sc, c_avv: ve})
        slb0K = [0]
        slb0V = [0]
        slb0Q = [0]
        slb0U = [0]
        aod0 = [[0], [0]]

        def a1_body(P):
            def MMG(bank, lhsT_fn, rhs_fn, label):
                for k in range(16):
                    fn = (lambda k=k: te.matmul(bank, lhsT_fn(k), rhs_fn(k),
                                                start=(k == 0), stop=(k == 15)))
                    if k == 15:
                        P.op(c_amm, fn, label)
                    else:
                        P.raw(fn)
            P.wait_ext(te, c_slK, 16, 16, slb0K)
            for t in range(16):
                if t >= 2:
                    P.wait(te, f"evK{t-2}")
                else:
                    P.wait(te, f"evQ{14+t}", dit=-1)
                MMG(pf[t % 2][:, :],
                    lambda k, t=t: xTb[:, k, t * 128:(t + 1) * 128],
                    lambda k: SL0[:, k, :], f"mmK{t}")
            P.wait_ext(te, c_slV, 16, 16, slb0V)
            for t in range(16):
                if t >= 2:
                    P.wait(te, f"evV{t-2}")
                else:
                    P.wait(te, f"vObB{12+t}", dit=-1)
                MMG(pf[2 + t % 2][:, :],
                    lambda k, t=t: xTb[:, k, t * 128:(t + 1) * 128],
                    lambda k: SL1[:, k, :], f"mmV{t}")
            P.wait(te, "evK15")
            P.wait(te, "evV15")
            P.wait(te, "vObB15", dit=-1)
            for hh in range(4):
                MMG(pf[4][:, hh * 128:(hh + 1) * 128],
                    lambda lt, hh=hh: KQ[:, lt, hh * 128:(hh + 1) * 128],
                    lambda lt, hh=hh: VU[:, lt, hh * 128:(hh + 1) * 128],
                    f"mmKV{hh}")
            P.wait_ext(te, c_slQ, 16, 16, slb0Q)
            for qg in range(16):
                hh, n = divmod(qg, 4)
                if qg >= 2:
                    P.wait(te, f"evQ{qg-2}")
                MMG(pf[qg % 2][:, :],
                    lambda k, hh=hh: SL0[:, k, hh * 128:(hh + 1) * 128],
                    lambda k, n=n: xTb[:, k, n * 512:(n + 1) * 512], f"mmQ{qg}")
            P.wait_ext(te, c_slU, 16, 16, slb0U)
            for t in range(16):
                if t >= 2:
                    P.wait(te, f"evU{t-2}")
                MMG(pf[2 + t % 2][:, :],
                    lambda k, t=t: xTb[:, k, t * 128:(t + 1) * 128],
                    lambda k: SL1[:, k, :], f"mmU{t}")
            P.wait(te, "evKV")
            P.wait(te, "evQ15")
            P.wait(te, "evU15")
            for j in range(5):
                if j < 4:
                    hh = j
                    for t in range(16):
                        oc = hh * 16 + t
                        qb = oc // 4
                        bank = pf[2 + qb % 4]
                        col = (oc % 4) * 128
                        if t % 4 == 0:
                            if qb >= 4:
                                P.wait(te, f"vObB{qb-4}")
                            else:
                                P.wait(te, f"vObB{qb+12}", dit=-1)
                        P.op(c_amm, (lambda hh=hh, t=t, bank=bank, col=col:
                                     te.matmul(
                            bank[:, col:col + 128],
                            QT[:, hh, t * 128:(t + 1) * 128],
                            KVb[:, hh * 128:(hh + 1) * 128],
                            start=True, stop=True)), f"mmO{oc}")
                if j >= 1:
                    hh = j - 1
                    for t in range(16):
                        oc = hh * 16 + t
                        qb = oc // 4
                        if t % 4 == 0:
                            P.wait(te, f"vOfB{qb}")
                        if oc % 8 == 0:
                            if oc == 0:
                                P.wait(te, "vOTb7", dit=-1)
                            else:
                                P.wait(te, f"vOTb{oc // 8 - 1}")
                        P.op(c_amm, (lambda t=t, qb=qb: te.transpose(
                            pbt[:, (t % 8) * 128:(t % 8) * 128 + 128],
                            bvv(o_Of[qb % 2], 512)[:, (t % 4) * 128:(t % 4) * 128 + 128],
                            idb[:, :])), f"mmT{oc}")

            for t in range(16):
                P.wait(sc, f"mmK{t}")
                P.op(c_asc, (lambda t=t: sc.activation(
                    KQ[:, t, :], pf[t % 2][:, :], AF.Relu,
                    scale=cst[:, C_ISCALE:C_ISCALE + 1])), f"evK{t}")
            for qg in range(16):
                hh, n = divmod(qg, 4)
                P.wait(sc, f"mmQ{qg}")
                P.op(c_asc, (lambda hh=hh, n=n, qg=qg: sc.activation(
                    QT[:, hh, n * 512:(n + 1) * 512], pf[qg % 2][:, :],
                    AF.Relu, scale=cst[:, C_ISCALE:C_ISCALE + 1])), f"evQ{qg}")
            for t in range(16):
                P.wait(sc, f"mmU{t}")
                P.op(c_asc, (lambda t=t: sc.activation(
                    VU[:, t, :], pf[2 + t % 2][:, :], AF.Silu)), f"evU{t}")
            for hh in range(4):
                half = (hh % 2) * 16
                P.wait(sc, f"vSS{hh}")
                if hh >= 2:
                    P.wait(sc, f"vOfB{(hh - 2) * 4 + 3}")
                else:
                    P.wait(sc, f"vOfB{(hh + 2) * 4 + 3}", dit=-1)
                P.op(c_asc, (lambda half=half: sc.activation(
                    ssc[:, half:half + 16], ssb[:, half:half + 16], AF.Ln,
                    bias=cst[:, C_SRMS_EPS:C_SRMS_EPS + 1])), f"scL{hh}")
                P.wait(sc, f"scL{hh}")
                P.op(c_asc, (lambda half=half: sc.activation(
                    ssc[:, half:half + 16], ssc[:, half:half + 16], AF.Exp,
                    scale=cst[:, C_NEGH:C_NEGH + 1],
                    bias=cst[:, C_SRMS_LNC:C_SRMS_LNC + 1])), f"scE{hh}")

            for t in range(16):
                P.wait(ve, f"mmV{t}")
                P.op(c_avv, (lambda t=t: ve.tensor_copy(
                    VU[:, t, :], pf[2 + t % 2][:, :])), f"evV{t}")
            P.wait(ve, "mmKV3")
            P.op(c_avv, (lambda: ve.tensor_copy(KVb[:, :], pf[4][:, :])), "evKV")
            for j in range(5):
                if j < 4:
                    hh = j
                    half = (hh % 2) * 16
                    for tb in range(4):
                        oc0 = hh * 16 + tb * 4
                        qb = oc0 // 4
                        bank = pf[2 + qb % 4]
                        P.wait(ve, f"mmO{oc0 + 3}")
                        if tb == 0:
                            if hh >= 2:
                                P.wait(ve, f"scL{hh-2}")
                            else:
                                P.wait(ve, f"scL{hh+2}", dit=-1)
                        P.op(c_avv, (lambda tb=tb, bank=bank, hp=hh % 2:
                                     ve.tensor_copy(
                            Obb2[hp][:, tb * 512:(tb + 1) * 512],
                            bank[:, :])), f"vObB{qb}")
                        P.wait_self(ve, c_avv)
                        P.op(c_avv, (lambda tb=tb, p=tb % 2, hp=hh % 2:
                                     ve.tensor_tensor(
                            fvv(o_osq[p], 512),
                            Obb2[hp][:, tb * 512:(tb + 1) * 512],
                            Obb2[hp][:, tb * 512:(tb + 1) * 512], ALU.mult)),
                            None)
                        P.wait_self(ve, c_avv)
                        P.op(c_avv, (lambda tb=tb, p=tb % 2, half=half:
                                     ve.reduce_sum(
                            bass.AP(ssb.tensor, half + tb * 4,
                                    [[32, 128], [1, 4]]),
                            bass.AP(ARf.tensor, o_osq[p],
                                    [[ARN, 128], [128, 4], [1, 128]]),
                            AX.X)), f"vSS{hh}" if tb == 3 else None)
                if j >= 1:
                    hh = j - 1
                    half = (hh % 2) * 16
                    for tb in range(4):
                        oc0 = hh * 16 + tb * 4
                        qb = oc0 // 4
                        if tb == 0:
                            P.wait(ve, f"scE{hh}")
                        if qb >= 2:
                            P.wait(ve, f"mmT{(qb - 2) * 4 + 3}")
                        else:
                            P.wait(ve, f"mmT{(qb + 14) * 4 + 3}", dit=-1)
                        sscrep = bass.AP(ssc.tensor, half + tb * 4,
                                         [[32, 128], [1, 4], [0, 128]])
                        ofb3 = bass.AP(ARb.tensor, 2 * o_Of[qb % 2],
                                       [[2 * ARN, 128], [128, 4], [1, 128]])
                        obbin = bass.AP(ARb.tensor,
                                        2 * o_Ob + (hh % 2) * 2048 + tb * 512,
                                        [[2 * ARN, 128], [128, 4], [1, 128]])
                        vuin = bass.AP(ARb.tensor,
                                       2 * o_VU + tb * 4 * 512 + hh * 128,
                                       [[2 * ARN, 128], [512, 4], [1, 128]])
                        P.op(c_avv, (lambda ofb3=ofb3, obbin=obbin,
                                     sscrep=sscrep: ve.tensor_tensor(
                            ofb3, obbin, sscrep, ALU.mult)), None)
                        P.wait_self(ve, c_avv)
                        P.op(c_avv, (lambda ofb3=ofb3, vuin=vuin:
                                     ve.tensor_tensor(
                            ofb3, ofb3, vuin, ALU.mult)), f"vOfB{qb}")
                        if tb == 1 or tb == 3:
                            halfi = 0 if tb == 1 else 1
                            P.wait(ve, f"mmT{hh * 16 + (halfi + 1) * 8 - 1}")
                            if halfi == 0:
                                P.wait_ext(ve, c_aod[hh % 2], 32,
                                           16 if hh >= 2 else 0,
                                           aod0[hh % 2])
                            P.op(c_avv, (lambda hh=hh, halfi=halfi:
                                         ve.tensor_copy(
                                bvv(o_OTst[hh % 2], 2048)
                                [:, halfi * 1024:(halfi + 1) * 1024],
                                pbt[:, :])), f"vOTb{hh * 2 + halfi}")

        ph.begin()
        a1_body(ph)
        ph.mid()
        aod0[0][0] = c_aod[0].count
        aod0[1][0] = c_aod[1].count
        slb0K[0] = c_slK.count
        slb0V[0] = c_slV.count
        slb0Q[0] = c_slQ.count
        slb0U[0] = c_slU.count
        for g in range(4):
            if b + g > 0:
                gp.wait_ge(c_amm.sem, ph.val("mmQ15", g - 1))
            gp.dma_start(SL0, wkT.rearrange("(k p) e -> p k e", p=128)
                         [:, :, g * 512:(g + 1) * 512]).then_inc(c_slK.sem, 16)
            c_slK.count += 16
            if b + g > 0:
                gp.wait_ge(c_amm.sem, ph.val("mmU15", g - 1))
            gp.dma_start(SL1, wvT.rearrange("(k p) e -> p k e", p=128)
                         [:, :, g * 512:(g + 1) * 512]).then_inc(c_slV.sem, 16)
            c_slV.count += 16
            gp.wait_ge(c_amm.sem, ph.val("mmK15", g))
            gp.dma_start(SL0, wqT.rearrange("(k p) e -> p k e", p=128)
                         [:, :, g * 512:(g + 1) * 512]).then_inc(c_slQ.sem, 16)
            c_slQ.count += 16
            gp.wait_ge(c_amm.sem, ph.val("mmV15", g))
            gp.dma_start(SL1, wuT.rearrange("(k p) e -> p k e", p=128)
                         [:, :, g * 512:(g + 1) * 512]).then_inc(c_slU.sem, 16)
            c_slU.count += 16
            for hh in range(4):
                h = g * 4 + hh
                sy.wait_ge(c_avv.sem, ph.val(f"vOTb{hh * 2 + 1}", g))
                sy.dma_start(ot_hbm[b, h * 128:(h + 1) * 128, :],
                             bvv(o_OTst[hh % 2], 2048)).then_inc(c_aod[hh % 2].sem, 16)
                c_aod[hh % 2].count += 16
        ph.prepare(st)
        with nc.Fori(0, 4, engines=CENG) as gi:
            ph.i = gi
            a1_body(ph)
        ph.finish(4)
        nc.all_engine_barrier()
        if debug_phase == "A1":
            sy.wait_ge(c_aod[0].sem, c_aod[0].count)
            sy.wait_ge(c_aod[1].sem, c_aod[1].count)
            drain_all_dma()
            st.close()
            return nc

    ph2 = Phase(nc, {c_bmm: te, c_bv: ve, c_bsc: sc})
    ots0 = [[0], [0]]
    xrs0 = [[0], [0]]
    bx1f0 = [[0], [0]]
    bx1t0 = [[0], [0]]
    bcwt0 = [[0], [0]]

    def router(P, par):
        def V(fn, label=None, self_wait=True):
            if self_wait:
                P.wait_self(ve, c_bv)
            P.op(c_bv, fn, label)
        prob_h = rr["prob"].tensor
        wgt_h = rr["wgt"].tensor
        prim_h = rr["prim"].tensor
        Mx_h = rr["Mx"].tensor
        C2_h = rr["C2"].tensor
        p_qrep = bass.AP(prob_h, 0, [[32, 128], [1, 32], [0, 32]])
        p_erep = bass.AP(prob_h, 0, [[32, 128], [0, 32], [1, 32]])
        prim_srep = bass.AP(prim_h, 0, [[32, 128], [0, 32], [1, 32]])
        prim_emaj = bass.AP(prim_h, 0, [[32, 128], [1, 32], [0, 32]])
        mx_erep = bass.AP(Mx_h, 0, [[32, 128], [0, 32], [1, 32]])
        c2_erep = bass.AP(C2_h, 0, [[32, 128], [0, 32], [1, 32]])
        gt3 = bass.AP(candb.tensor, 0, [[1024, 128], [32, 32], [1, 32]])
        cd3 = bass.AP(candb.tensor, 0, [[1024, 128], [32, 32], [1, 32]])
        cdE3 = gt3
        simB3 = bass.AP(simB.tensor, 0, [[1024, 128], [32, 32], [1, 32]])
        simBE3 = bass.AP(simBE.tensor, 0, [[1024, 128], [32, 32], [1, 32]])

        V(lambda: ve.tensor_copy(rr["Lg"][:, :], pf[4][:, 0:32]),
          f"bvLG{par}", self_wait=False)
        V(lambda: ve.reduce_max(c1[:, :], rr["Lg"][:, :], AX.X))
        V(lambda: ve.tensor_scalar(rr["Lg"][:, :], rr["Lg"][:, :], c1[:, :],
                                   None, ALU.subtract), f"bvLS{par}")
        P.wait(ve, f"bscEX{par}")
        V(lambda: ve.tensor_tensor(gt3, p_erep, p_qrep, ALU.is_gt),
          self_wait=False)
        V(lambda: ve.reduce_sum(rr["rnk"][:, :], gt3, AX.X))
        V(lambda: ve.tensor_scalar(rr["top8"][:, :], rr["rnk"][:, :], 8.0,
                                   None, ALU.is_lt))
        V(lambda: ve.tensor_scalar(rr["prim"][:, :], rr["rnk"][:, :], 4.0,
                                   None, ALU.is_lt), self_wait=False)
        V(lambda: ve.tensor_tensor(rr["sec"][:, :], rr["top8"][:, :],
                                   rr["prim"][:, :], ALU.subtract))
        V(lambda: ve.tensor_tensor(rr["wgt"][:, :], rr["prob"][:, :],
                                   rr["top8"][:, :], ALU.mult), f"bvWG{par}")
        V(lambda: ve.reduce_sum(c1[:, :], rr["wgt"][:, :], AX.X))
        V(lambda: ve.tensor_scalar(c1[:, :], c1[:, :], 1e-12, None, ALU.max))
        V(lambda: ve.reciprocal(c2[:, :], c1[:, :]))
        V(lambda: ve.tensor_scalar(rr["wgt"][:, :], rr["wgt"][:, :], c2[:, :],
                                   None, ALU.mult))
        V(lambda: ve.tensor_tensor(cd3, prim_srep, simB3, ALU.mult))
        V(lambda: ve.reduce_max(rr["Mx"][:, :], cd3, AX.X))
        V(lambda: ve.tensor_scalar(rr["t3"][:, :], rr["Mx"][:, :], 0.5, None,
                                   ALU.is_ge))
        V(lambda: ve.tensor_scalar(rr["keep"][:, :], rr["t3"][:, :], 1.0, -1.0,
                                   ALU.subtract, ALU.mult))
        V(lambda: ve.tensor_tensor(rr["C2"][:, :], rr["wgt"][:, :],
                                   rr["sec"][:, :], ALU.mult), self_wait=False)
        V(lambda: ve.tensor_tensor(rr["t5"][:, :], rr["C2"][:, :],
                                   rr["keep"][:, :], ALU.mult))
        V(lambda: ve.tensor_tensor(rr["C2"][:, :], rr["C2"][:, :],
                                   rr["t3"][:, :], ALU.mult))
        V(lambda: ve.tensor_tensor(cdE3, prim_emaj, simBE3, ALU.mult))
        V(lambda: ve.tensor_tensor(gt3, cdE3, mx_erep, ALU.is_equal))
        V(lambda: ve.tensor_tensor(gt3, gt3, c2_erep, ALU.mult))
        V(lambda: ve.reduce_sum(rr["acc"][:, :], gt3, AX.X))
        P.wait(ve, f"bmmCWT{par}")
        V(lambda: ve.tensor_tensor(cwP[par][:, :], rr["wgt"][:, :],
                                   rr["prim"][:, :], ALU.mult))
        V(lambda: ve.tensor_tensor(cwP[par][:, :], cwP[par][:, :],
                                   rr["t5"][:, :], ALU.add))
        V(lambda: ve.tensor_tensor(cwP[par][:, :], cwP[par][:, :],
                                   rr["acc"][:, :], ALU.add), f"bvCW{par}")

    def a2_body(P):
        OTsl_ = [bvv(o_OTs[p], 2048).rearrange("p (k t) -> p k t", k=NK)
                 for p in range(2)]
        R1_ = [fvv(o_r1[p], 2048) for p in range(2)]
        SQ_ = [fvv(o_sq[p], 2048) for p in range(2)]
        X1_ = [fvv(o_x1[p], 2048) for p in range(2)]
        X1B_ = [bvv(o_x1b[p], 2048) for p in range(2)]
        X1TS_ = [bvv(o_x1ts[p], 2048) for p in range(2)]
        XR_ = [fvv(o_xres[p], 2048) for p in range(2)]
        CWT_ = [fvv(o_cwt[p], 128) for p in range(2)]

        for par in range(2):
            P.wait_ext(te, c_ots[par], 16, 16, ots0[par])
            for ec in range(4):
                gq = par * 4 + ec
                if gq >= 2:
                    P.wait(te, f"br1_{gq-2}")
                else:
                    P.wait(te, f"br1_{gq+6}", dit=-1)
                for k in range(16):
                    fn = (lambda ec=ec, k=k, gq=gq, par=par: te.matmul(
                        pf[gq % 2][:, :], OTsl_[par][:, k, :],
                        WOs[:, k, ec * 512:(ec + 1) * 512],
                        start=(k == 0), stop=(k == 15)))
                    if k == 15:
                        P.op(c_bmm, fn, f"bmmWO{gq}")
                    else:
                        P.raw(fn)
        def emit_tr(par):
            P.wait(te, f"bvX1B{par}")
            for k in range(16):
                if k == 8:
                    P.wait(te, f"bvTS{par}_0")
                elif k == 0:
                    if par == 0:
                        P.wait(te, "bvTS1_1", dit=-1)
                    else:
                        P.wait(te, "bvTS0_1")
                P.op(c_bmm, (lambda k=k, par=par: te.transpose(
                    pbt[:, (k % 8) * 128:(k % 8) * 128 + 128],
                    X1B_[par][:, k * 128:(k + 1) * 128], idb[:, :])),
                    f"bmmT{par}_{k}")

        def emit_lg(par):
            P.wait(te, f"bvTS{par}_1")
            if par == 0:
                P.wait(te, "bvLG1", dit=-1)
            else:
                P.wait(te, "bvLG0")
            for k in range(16):
                fn = (lambda k=k, par=par: te.matmul(
                    pf[4][:, 0:32], X1TS_[par][:, k * 128:(k + 1) * 128],
                    GTB[:, k, :], start=(k == 0), stop=(k == 15)))
                if k == 15:
                    P.op(c_bmm, fn, f"bmmLG{par}")
                else:
                    P.raw(fn)

        def emit_cwt(par):
            P.wait(te, f"bvCW{par}", dit=-1)
            if par == 0:
                P.wait(te, "bvCWT1", dit=-1)
            else:
                P.wait(te, "bvCWT0")
            P.op(c_bmm, (lambda par=par: te.transpose(
                pf[5][0:32, 0:128], cwP[par][:, :], idf[:, :])),
                f"bmmCWT{par}")

        emit_tr(0)
        emit_lg(0)
        emit_tr(1)
        emit_cwt(0)
        emit_cwt(1)
        emit_lg(1)

        for par in range(2):
            P.wait_ext(ve, c_xrs[par], 16, 16, xrs0[par])
            P.wait(ve, f"bscAP{par}", dit=-1)
            for ec in range(4):
                gq = par * 4 + ec
                P.wait(ve, f"bmmWO{gq}")
                P.op(c_bv, (lambda ec=ec, gq=gq, par=par: ve.tensor_tensor(
                    R1_[par][:, ec * 512:(ec + 1) * 512], pf[gq % 2][:, :],
                    XR_[par][:, ec * 512:(ec + 1) * 512], ALU.add)),
                    f"br1_{par*4+ec}")
            P.wait_self(ve, c_bv)
            P.op(c_bv, (lambda par=par: ve.reduce_sum(
                lnm[:, par:par + 1], R1_[par], AX.X)), None)
            P.wait_self(ve, c_bv)
            P.op(c_bv, (lambda par=par: ve.tensor_scalar(
                lnm[:, par:par + 1], lnm[:, par:par + 1], 1.0 / D, None,
                ALU.mult)), f"bvM{par}")
            P.op(c_bv, (lambda par=par: ve.tensor_tensor(
                SQ_[par], R1_[par], R1_[par], ALU.mult)), None)
            P.wait_self(ve, c_bv)
            P.op(c_bv, (lambda par=par: ve.reduce_sum(
                lnv[:, par:par + 1], SQ_[par], AX.X)), None)
            P.wait_self(ve, c_bv)
            P.op(c_bv, (lambda par=par: ve.tensor_scalar(
                lnv[:, par:par + 1], lnv[:, par:par + 1], 1.0 / D, None,
                ALU.mult)), None)
            P.op(c_bv, (lambda par=par: ve.tensor_tensor(
                c2[:, :], lnm[:, par:par + 1], lnm[:, par:par + 1], ALU.mult)),
                None)
            P.wait_self(ve, c_bv)
            P.op(c_bv, (lambda par=par: ve.tensor_tensor(
                lnv[:, par:par + 1], lnv[:, par:par + 1], c2[:, :],
                ALU.subtract)), f"bvVAR{par}")
            P.wait(ve, f"bscRS{par}")
            P.op(c_bv, (lambda par=par: ve.tensor_scalar(
                lnb[:, par:par + 1], lnm[:, par:par + 1], lnr[:, par:par + 1],
                -1.0, ALU.mult, ALU.mult)), f"bvNMR{par}")
            P.wait(ve, f"bscAP{par}")
            P.op(c_bv, (lambda par=par: ve.tensor_copy(X1B_[par], X1_[par])),
                 f"bvX1B{par}")
        for par in range(2):
            P.wait(ve, f"bmmT{par}_7")
            P.wait_ext(ve, c_bx1t[par], 16, 0, bx1t0[par])
            P.op(c_bv, (lambda par=par: ve.tensor_copy(
                X1TS_[par][:, 0:1024], pbt[:, :])), f"bvTS{par}_0")
            P.wait(ve, f"bmmT{par}_15")
            P.op(c_bv, (lambda par=par: ve.tensor_copy(
                X1TS_[par][:, 1024:2048], pbt[:, :])), f"bvTS{par}_1")
        for par in range(2):
            P.wait(ve, f"bmmCWT{par}")
            P.wait_ext(ve, c_bcwt[par], 16, -16, bcwt0[par])
            P.op(c_bv, (lambda par=par: ve.tensor_copy(
                CWT_[par][0:32, :], pf[5][0:32, 0:128])), f"bvCWT{par}")
        for par in range(2):
            P.wait(ve, f"bmmLG{par}")
            router(P, par)

        for par in range(2):
            P.wait(sc, f"bvVAR{par}")
            P.op(c_bsc, (lambda par=par: sc.activation(
                lnr[:, par:par + 1], lnv[:, par:par + 1], AF.Ln,
                bias=cst[:, C_EPS_LN:C_EPS_LN + 1])), None)
            P.wait_self(sc, c_bsc)
            P.op(c_bsc, (lambda par=par: sc.activation(
                lnr[:, par:par + 1], lnr[:, par:par + 1], AF.Exp,
                scale=cst[:, C_NEGH:C_NEGH + 1])), f"bscRS{par}")
            P.wait(sc, f"bvNMR{par}")
            P.wait_ext(sc, c_bx1f[par], 16, 0, bx1f0[par])
            P.op(c_bsc, (lambda par=par: sc.activation(
                X1_[par], R1_[par], AF.Identity, scale=lnr[:, par:par + 1],
                bias=lnb[:, par:par + 1])), f"bscAP{par}")
        for par in range(2):
            P.wait(sc, f"bvLS{par}")
            if par == 0:
                P.wait(sc, "bvWG1", dit=-1)
            else:
                P.wait(sc, "bvWG0")
            P.op(c_bsc, (lambda: sc.activation(
                rr["prob"][:, :], rr["Lg"][:, :], AF.Exp)), f"bscEX{par}")

    ph2.begin()
    a2_body(ph2)
    ph2.mid()
    for p in range(2):
        ots0[p][0] = c_ots[p].count
        xrs0[p][0] = c_xrs[p].count
        bx1f0[p][0] = c_bx1f[p].count
        bx1t0[p][0] = c_bx1t[p].count
        bcwt0[p][0] = c_bcwt[p].count
    gp.dma_start(WOs, woT.rearrange("(k p) e -> p k e", p=128)).then_inc(c_preg.sem, 16)
    c_preg.count += 16
    sy.wait_ge(c_aod[0].sem, c_aod[0].count)
    sy.wait_ge(c_aod[1].sem, c_aod[1].count)
    for pi in range(16):
        for par in range(2):
            tt = 2 * pi + par
            b = tt // 16
            t = tt % 16
            if tt >= 2:
                sy.wait_ge(c_bmm.sem, ph2.val(f"bmmWO{(par * 4) + 3}", pi - 1))
            sy.dma_start(bvv(o_OTs[par], 2048).rearrange("p (k t) -> p k t", k=NK),
                         ot_hbm[b].rearrange("(k p) t -> p k t", p=128)
                         [:, :, t * 128:(t + 1) * 128]).then_inc(c_ots[par].sem, 16)
            c_ots[par].count += 16
            if tt >= 2:
                sy.wait_ge(c_bv.sem, ph2.val(f"br1_{par * 4 + 3}", pi - 1))
            sy.dma_start(fvv(o_xres[par], 2048),
                         xin[tt * 128:(tt + 1) * 128, :]).then_inc(c_xrs[par].sem, 16)
            c_xrs[par].count += 16
        for par in range(2):
            tt = 2 * pi + par
            sy.wait_ge(c_bsc.sem, ph2.val(f"bscAP{par}", pi))
            sy.dma_start(x1f[tt * 128:(tt + 1) * 128, :],
                         fvv(o_x1[par], 2048)).then_inc(c_bx1f[par].sem, 16)
            c_bx1f[par].count += 16
            sy.wait_ge(c_bv.sem, ph2.val(f"bvTS{par}_1", pi))
            sy.dma_start(x1t.rearrange("(k p) t -> p k t", p=128)
                         [:, :, tt * 128:(tt + 1) * 128],
                         bvv(o_x1ts[par], 2048).rearrange("p (k t) -> p k t", k=NK)
                         ).then_inc(c_bx1t[par].sem, 16)
            c_bx1t[par].count += 16
            if tt >= 2:
                sy.wait_ge(c_bv.sem, ph2.val(f"bvCWT{par}", pi))
                sy.dma_start(cwTd[:, (tt - 2) * 128:(tt - 1) * 128],
                             fvv(o_cwt[par], 128)[0:32, :]
                             ).then_inc(c_bcwt[par].sem, 16)
                c_bcwt[par].count += 16
    te.wait_ge(c_preg.sem, c_preg.count)
    ph2.prepare(st)
    with nc.Fori(0, 16, engines=CENG) as pi2:
        ph2.i = pi2
        a2_body(ph2)
    ph2.finish(16)
    for par in range(2):
        te.wait_ge(c_bv.sem, ph2.val(f"bvCW{par}", 15))
        if par == 0:
            te.wait_ge(c_bv.sem, ph2.val("bvCWT1", 15))
        else:
            te.wait_ge(c_bv.sem, c_bv.count)
        te.transpose(pf[5][0:32, 0:128],
                     (rr["cw"] if par == 0 else rr["cw1"])[:, :],
                     idf[:, :]).then_inc(c_bmm.sem, 1)
        c_bmm.count += 1
        ve.wait_ge(c_bmm.sem, c_bmm.count)
        ve.wait_ge(c_bcwt[par].sem, c_bcwt[par].count)
        ve.tensor_copy(fvv(o_cwt[par], 128)[0:32, :],
                       pf[5][0:32, 0:128]).then_inc(c_bv.sem, 1)
        c_bv.count += 1
        sy.wait_ge(c_bv.sem, c_bv.count)
        sy.dma_start(cwTd[:, (30 + par) * 128:(31 + par) * 128],
                     fvv(o_cwt[par], 128)[0:32, :]
                     ).then_inc(c_bcwt[par].sem, 16)
        c_bcwt[par].count += 16
    nc.all_engine_barrier()

    if debug_phase == "A":
        for p in range(2):
            sy.wait_ge(c_bx1f[p].sem, c_bx1f[p].count)
            sy.wait_ge(c_bx1t[p].sem, c_bx1t[p].count)
            sy.wait_ge(c_bcwt[p].sem, c_bcwt[p].count)
        drain_all_dma()
        st.close()
        return nc

    ph3 = Phase(nc, {c_cmm: te, c_csc: sc, c_cv: ve})
    crow0 = [0]
    cx10 = [[0], [0]]
    cgd0 = [[0], [0]]
    chd0 = [[0], [0]]

    def c_body(P):
        for par in range(2):
            W = WGU[par]
            CWR = cwrow[:, 0:T]
            P.wait_ext(te, c_cgd[par], 32, 32, cgd0[par])
            P.wait_ext(te, c_crow, 32, (par + 1) * 16, crow0)
            for n in range(8):
                np_ = n % 2
                XC = X1C[np_]
                P.wait_ext(te, c_cx1[n % 2], 128,
                           par * 64 + (n // 2 + 1) * 16, cx10[n % 2])
                if n >= 2:
                    P.wait(te, f"cvH{par}_{n-2}_5")
                else:
                    P.wait(te, f"cvH{1-par}_{n+6}_5", dit=(-1 if par == 0 else 0))
                P.op(c_cmm, (lambda CWR=CWR, n=n, np_=np_: te.matmul(
                    pf[4 + np_][:, :], ones1[:, :], CWR[:, n * 512:(n + 1) * 512],
                    start=True, stop=True)), f"cmmB{par}_{n}")
                for mp in range(6):
                    gidx = n * 6 + mp
                    if gidx >= 2:
                        P.wait(te, f"cvT1{par}_{gidx-2}")
                    else:
                        P.wait(te, f"cvT1{1-par}_{gidx+46}",
                               dit=(-1 if par == 0 else 0))
                    for k in range(16):
                        fn = (lambda k=k, mp=mp, n=n, gidx=gidx, W=W, XC=XC:
                              te.matmul(pf[gidx % 2][:, :],
                                        W[:, k, mp * 128:(mp + 1) * 128],
                                        XC[:, k, :],
                                        start=(k == 0), stop=(k == 15)))
                        if k == 15:
                            P.op(c_cmm, fn, f"cmmG{par}_{gidx}")
                        else:
                            P.raw(fn)
                    for k in range(16):
                        fn = (lambda k=k, mp=mp, n=n, gidx=gidx, W=W, XC=XC:
                              te.matmul(pf[2 + gidx % 2][:, :],
                                        W[:, k, 768 + mp * 128:768 + (mp + 1) * 128],
                                        XC[:, k, :],
                                        start=(k == 0), stop=(k == 15)))
                        if k == 15:
                            P.op(c_cmm, fn, f"cmmU{par}_{gidx}")
                        else:
                            P.raw(fn)
            for n in range(8):
                for mp in range(6):
                    gidx = n * 6 + mp
                    P.wait(sc, f"cmmG{par}_{gidx}")
                    P.op(c_csc, (lambda gidx=gidx: sc.activation(
                        fvv(o_sg[gidx % 2], 512), pf[gidx % 2][:, :], AF.Silu)),
                        f"csg{par}_{gidx}")
            for n in range(8):
                np_ = n % 2
                for mp in range(6):
                    gidx = n * 6 + mp
                    P.wait(ve, f"cmmU{par}_{gidx}")
                    P.wait(ve, f"csg{par}_{gidx}")
                    P.op(c_cv, (lambda gidx=gidx: ve.tensor_tensor(
                        fvv(o_t1[gidx % 2], 512), pf[2 + gidx % 2][:, :],
                        fvv(o_sg[gidx % 2], 512), ALU.mult)),
                        f"cvT1{par}_{gidx}")
                    P.wait(ve, f"cmmB{par}_{n}")
                    if mp == 0:
                        P.wait_ext(ve, c_chd[n % 2], 128,
                                   par * 64 + (n // 2) * 16, chd0[n % 2])
                    P.wait_self(ve, c_cv)
                    P.op(c_cv, (lambda gidx=gidx, mp=mp, np_=np_: ve.tensor_tensor(
                        HST[np_][:, mp, :], fvv(o_t1[gidx % 2], 512),
                        pf[4 + np_][:, :], ALU.mult)), f"cvH{par}_{n}_{mp}")

    ph3.begin()
    c_body(ph3)
    ph3.mid()
    crow0[0] = c_crow.count
    for p in range(2):
        cx10[p][0] = c_cx1[p].count
        cgd0[p][0] = c_cgd[p].count
        chd0[p][0] = c_chd[p].count
    for p in range(2):
        for chx in (c_bx1f[p], c_bx1t[p], c_bcwt[p]):
            sy.wait_ge(chx.sem, chx.count)
            gp.wait_ge(chx.sem, chx.count)
            ve.wait_ge(chx.sem, chx.count)
            sc.wait_ge(chx.sem, chx.count)
    for e in range(NEXP):
        pi, par = divmod(e, 2)
        if e >= 2:
            gp.wait_ge(c_cmm.sem, ph3.val(f"cmmU{par}_47", pi - 1))
        gp.dma_start(WGU[par][:, :, 0:768],
                     wguTd[e].rearrange("(k p) m -> p k m", p=128)[:, :, 0:768]
                     ).then_inc(c_cgd[par].sem, 16)
        gp.dma_start(WGU[par][:, :, 768:1536],
                     wguTd[e].rearrange("(k p) m -> p k m", p=128)[:, :, 768:1536]
                     ).then_inc(c_cgd[par].sem, 16)
        c_cgd[par].count += 32
        for q in range(4):
            gp.dma_start(bvv(o_wdc[q % 2], 3072).rearrange("p (k m) -> p k m", k=6),
                         wdTd[e].rearrange("(k p) m -> p k m", p=128)
                         [:, :, q * 512:(q + 1) * 512]).then_inc(c_cwc.sem, 16)
            c_cwc.count += 16
            gp.wait_ge(c_cwc.sem, c_cwc.count)
            gp.dma_start(wd_b[e].rearrange("(k p) m -> p k m", p=128)
                         [:, :, q * 512:(q + 1) * 512],
                         bvv(o_wdc[q % 2], 3072).rearrange("p (k m) -> p k m", k=6)
                         ).then_inc(c_cwc.sem, 16)
            c_cwc.count += 16
            gp.wait_ge(c_cwc.sem, c_cwc.count)
        if e >= 1:
            sy.wait_ge(c_cmm.sem,
                       ph3.val(f"cmmB{1 - par}_7", pi if par == 1 else pi - 1))
        sy.dma_start(cwrow[:, 0:T],
                     cwTd[e:e + 1, :]).then_inc(c_crow.sem, 16)
        c_crow.count += 16
        def c_load(n):
            gni = e * 8 + n
            if gni >= 2:
                sy.wait_ge(c_cmm.sem,
                           ph3.val(f"cmmU{(gni - 2) % 16 // 8}_{((gni - 2) % 8) * 6 + 5}",
                                   (gni - 2) // 16))
            sy.dma_start(X1C[n % 2],
                         x1t.rearrange("(k p) t -> p k t", p=128)
                         [:, :, n * 512:(n + 1) * 512]).then_inc(c_cx1[n % 2].sem, 16)
            c_cx1[n % 2].count += 16

        def c_store(n):
            sy.wait_ge(c_cv.sem, ph3.val(f"cvH{par}_{n}_5", pi))
            sy.dma_start(hbuf[e].rearrange("(k p) t -> p k t", p=128)
                         [:, :, n * 512:(n + 1) * 512],
                         HST[n % 2]).then_inc(c_chd[n % 2].sem, 16)
            c_chd[n % 2].count += 16

        for n in range(8):
            c_load(n)
            if n >= 1:
                c_store(n - 1)
        c_store(7)
    ph3.prepare(st)
    with nc.Fori(0, 16, engines=CENG) as ci:
        ph3.i = ci
        c_body(ph3)
    ph3.finish(16)
    nc.all_engine_barrier()

    if debug_phase == "C":
        sy.wait_ge(c_chd[0].sem, c_chd[0].count)
        sy.wait_ge(c_chd[1].sem, c_chd[1].count)
        drain_all_dma()
        st.close()
        return nc

    chd_full = [c_chd[0].count, c_chd[1].count]
    cwc_full = c_cwc.count
    for s in range(NSW):
        ph4 = Phase(nc, {c_dmm: te, c_dv: ve})
        dh0 = [[0], [0]]
        dgd0 = [[0], [0]]

        def d_body(P, s=s):
            for par in range(2):
                WD = WDS[par]
                P.wait_ext(te, c_dgd[par], 16, 16, dgd0[par])
                for n in range(8):
                    np_ = n % 2
                    P.wait_ext(te, c_dh[n % 2], 128,
                               par * 64 + (n // 2 + 1) * 16, dh0[n % 2])
                    for m in range(4):
                        dg = n * 4 + m
                        bank = pf[dg % 4]
                        if dg >= 4:
                            P.wait(te, f"dv{par}_{dg-4}")
                        else:
                            P.wait(te, f"dv{1-par}_{dg+28}",
                                   dit=(-1 if par == 0 else 0))
                        for k in range(6):
                            fn = (lambda k=k, m=m, np_=np_, WD=WD, bank=bank:
                                  te.matmul(bank[:, :],
                                            WD[:, k, m * 128:(m + 1) * 128],
                                            HC[np_][:, k, :],
                                            start=(k == 0), stop=(k == 5)))
                            if k == 5:
                                P.op(c_dmm, fn, f"dmm{par}_{dg}")
                            else:
                                P.raw(fn)
                for n in range(8):
                    for m in range(4):
                        dg = n * 4 + m
                        bank = pf[dg % 4]
                        P.wait(ve, f"dmm{par}_{dg}")
                        P.op(c_dv, (lambda n=n, m=m, bank=bank: ve.tensor_tensor(
                            fvv(o_y[m], 4096)[:, n * 512:(n + 1) * 512],
                            fvv(o_y[m], 4096)[:, n * 512:(n + 1) * 512],
                            bank[:, :], ALU.add)), f"dv{par}_{dg}")

        if s == 0:
            for m in range(4):
                ve.memset(fvv(o_y[m], 4096), 0.0).then_inc(c_dv.sem, 1)
                c_dv.count += 1

        ph4.begin()
        d_body(ph4)
        ph4.mid()
        for p in range(2):
            dh0[p][0] = c_dh[p].count
            dgd0[p][0] = c_dgd[p].count
        gp.wait_ge(c_cwc.sem, cwc_full)
        sy.wait_ge(c_chd[0].sem, chd_full[0])
        sy.wait_ge(c_chd[1].sem, chd_full[1])
        for p in range(2):
            sy.wait_ge(c_bx1f[p].sem, c_bx1f[p].count)
        for e in range(NEXP):
            pi, par = divmod(e, 2)
            if e >= 2:
                gp.wait_ge(c_dmm.sem, ph4.val(f"dmm{par}_31", pi - 1))
            gp.dma_start(WDS[par],
                         wd_b[e].rearrange("(k p) m -> p k m", p=128)
                         [:, :, s * 512:(s + 1) * 512]).then_inc(c_dgd[par].sem, 16)
            c_dgd[par].count += 16
            for n in range(8):
                gni = e * 8 + n
                if gni >= 2:
                    sy.wait_ge(c_dmm.sem,
                               ph4.val(f"dmm{(gni - 2) % 16 // 8}_{((gni - 2) % 8) * 4 + 3}",
                                       (gni - 2) // 16))
                sy.dma_start(HC[n % 2],
                             hbuf[e].rearrange("(k p) t -> p k t", p=128)
                             [:, :, n * 512:(n + 1) * 512]).then_inc(c_dh[n % 2].sem, 16)
                c_dh[n % 2].count += 16
        ph4.prepare(st)
        with nc.Fori(0, 16, engines=CENG) as di:
            ph4.i = di
            d_body(ph4)
        ph4.finish(16)

        dmm_full = c_dmm.count
        tail_v0 = c_dv.count
        tail_t0 = c_dmm.count
        drd_prev = [c_drd[0].count, c_drd[1].count]
        for ttl in range(NT):
            par = ttl % 2
            bank = pf[ttl % 6]
            if ttl >= 2:
                sy.wait_ge(c_dv.sem, tail_v0 + (ttl - 2) + 1)
            sy.dma_start(fvv(o_x1ft[par], 512),
                         x1f[ttl * 128:(ttl + 1) * 128, s * 512:(s + 1) * 512]
                         ).then_inc(c_dxf[par].sem, 16)
            c_dxf[par].count += 16
            if ttl == 0:
                te.wait_ge(c_dv.sem, tail_v0)
            if ttl >= 6:
                te.wait_ge(c_dv.sem, tail_v0 + (ttl - 6) + 1)
            for m in range(4):
                te.transpose(bank[:, m * 128:(m + 1) * 128],
                             fvv(o_y[m], 4096)[:, ttl * 128:(ttl + 1) * 128],
                             idf[:, :]).then_inc(c_dmm.sem, 1)
                c_dmm.count += 1
            ve.wait_ge(c_dmm.sem, tail_t0 + (ttl + 1) * 4)
            ve.wait_ge(c_dxf[par].sem, c_dxf[par].count)
            ve.wait_ge(c_drd[par].sem, drd_prev[par])
            ve.tensor_tensor(fvv(o_rstD + ttl * 512, 512), bank[:, :],
                             fvv(o_x1ft[par], 512), ALU.add).then_inc(c_dv.sem, 1)
            c_dv.count += 1
            sy.wait_ge(c_dv.sem, tail_v0 + ttl + 1)
            sy.dma_start(rres[ttl * 128:(ttl + 1) * 128, s * 512:(s + 1) * 512],
                         fvv(o_rstD + ttl * 512, 512)).then_inc(c_drd[par].sem, 16)
            c_drd[par].count += 16
        if s < NSW - 1:
            ve.wait_ge(c_dmm.sem, c_dmm.count)
            for m in range(4):
                ve.memset(fvv(o_y[m], 4096), 0.0).then_inc(c_dv.sem, 1)
                c_dv.count += 1
        dvm_after_memset = c_dv.count
        for ttl in range(NT):
            par = ttl % 2
            ve.wait_ge(c_dv.sem, c_dv.count)
            ve.reduce_sum(c1[:, :], fvv(o_rstD + ttl * 512, 512), AX.X
                          ).then_inc(c_dv.sem, 1)
            c_dv.count += 1
            ve.wait_ge(c_dv.sem, c_dv.count)
            ve.tensor_tensor(statm[:, ttl:ttl + 1], statm[:, ttl:ttl + 1],
                             c1[:, :], ALU.add).then_inc(c_dv.sem, 1)
            c_dv.count += 1
            ve.tensor_tensor(fvv(o_sqd[par], 512), fvv(o_rstD + ttl * 512, 512),
                             fvv(o_rstD + ttl * 512, 512), ALU.mult
                             ).then_inc(c_dv.sem, 1)
            c_dv.count += 1
            ve.wait_ge(c_dv.sem, c_dv.count)
            ve.reduce_sum(c2[:, :], fvv(o_sqd[par], 512), AX.X).then_inc(c_dv.sem, 1)
            c_dv.count += 1
            ve.wait_ge(c_dv.sem, c_dv.count)
            ve.tensor_tensor(stats[:, ttl:ttl + 1], stats[:, ttl:ttl + 1],
                             c2[:, :], ALU.add).then_inc(c_dv.sem, 1)
            c_dv.count += 1
        if s < NSW - 1:
            te.wait_ge(c_dv.sem, dvm_after_memset)
            sy.wait_ge(c_dmm.sem, c_dmm.count)
            gp.wait_ge(c_dmm.sem, c_dmm.count)
    ed0 = [c_ed[0].count, c_ed[1].count]
    ev0 = c_ev.count
    es0 = c_esc.count
    eo0 = [c_out[0].count, c_out[1].count]
    sy.wait_ge(c_drd[0].sem, c_drd[0].count)
    sy.wait_ge(c_drd[1].sem, c_drd[1].count)
    for ttl in range(NT):
        par = ttl % 2
        if ttl >= 2:
            sy.wait_ge(c_esc.sem, es0 + (ttl - 2) * 3 + 3)
        sy.dma_start(fvv(o_rrE[par], 2048),
                     rres[ttl * 128:(ttl + 1) * 128, :]).then_inc(c_ed[par].sem, 16)
        c_ed[par].count += 16
        ve.tensor_scalar(lnm[:, par:par + 1], statm[:, ttl:ttl + 1], 1.0 / D,
                         None, ALU.mult).then_inc(c_ev.sem, 1)
        c_ev.count += 1
        ve.tensor_scalar(c1[:, :], stats[:, ttl:ttl + 1], 1.0 / D,
                         None, ALU.mult).then_inc(c_ev.sem, 1)
        c_ev.count += 1
        ve.wait_ge(c_ev.sem, c_ev.count)
        ve.tensor_tensor(c2[:, :], lnm[:, par:par + 1], lnm[:, par:par + 1],
                         ALU.mult).then_inc(c_ev.sem, 1)
        c_ev.count += 1
        ve.wait_ge(c_ev.sem, c_ev.count)
        ve.tensor_tensor(lnv[:, par:par + 1], c1[:, :], c2[:, :],
                         ALU.subtract).then_inc(c_ev.sem, 1)
        c_ev.count += 1
        sc.wait_ge(c_ev.sem, c_ev.count)
        sc.activation(lnr[:, par:par + 1], lnv[:, par:par + 1], AF.Ln,
                      bias=cst[:, C_EPS_LN:C_EPS_LN + 1]).then_inc(c_esc.sem, 1)
        c_esc.count += 1
        sc.wait_ge(c_esc.sem, c_esc.count)
        sc.activation(lnr[:, par:par + 1], lnr[:, par:par + 1], AF.Exp,
                      scale=cst[:, C_NEGH:C_NEGH + 1]).then_inc(c_esc.sem, 1)
        c_esc.count += 1
        ve.wait_ge(c_esc.sem, c_esc.count)
        ve.tensor_scalar(lnb[:, par:par + 1], lnm[:, par:par + 1],
                         lnr[:, par:par + 1], -1.0, ALU.mult, ALU.mult
                         ).then_inc(c_ev.sem, 1)
        c_ev.count += 1
        sc.wait_ge(c_ev.sem, c_ev.count)
        sc.wait_ge(c_esc.sem, c_esc.count)
        sc.wait_ge(c_ed[par].sem, c_ed[par].count)
        if ttl >= 2:
            sc.wait_ge(c_out[par].sem, c_out[par].count)
        sc.activation(fvv(o_oe[par], 2048), fvv(o_rrE[par], 2048), AF.Identity,
                      scale=lnr[:, par:par + 1], bias=lnb[:, par:par + 1]
                      ).then_inc(c_esc.sem, 1)
        c_esc.count += 1
        sy.wait_ge(c_esc.sem, c_esc.count)
        sy.dma_start(out[ttl * 128:(ttl + 1) * 128, :],
                     fvv(o_oe[par], 2048)).then_inc(c_out[par].sem, 16)
        c_out[par].count += 16
    sy.wait_ge(c_out[0].sem, c_out[0].count)
    sy.wait_ge(c_out[1].sem, c_out[1].count)
    drain_all_dma()
    st.close()
    return nc


_NC_CACHE = {}


def kernel(L0_attn_x, L0_attn_wq, L0_attn_wk, L0_attn_wv, L0_attn_wu,
           L0_attn_wo, L0_ffn_gate_weight, L0_ffn_gate_up_proj,
           L0_ffn_down_proj, L0_ffn_similarity):
    x = np.ascontiguousarray(np.asarray(L0_attn_x, dtype=np.float32))
    wq = np.asarray(L0_attn_wq, dtype=np.float32)
    wk = np.asarray(L0_attn_wk, dtype=np.float32)
    wv = np.asarray(L0_attn_wv, dtype=np.float32)
    wu = np.asarray(L0_attn_wu, dtype=np.float32)
    wo = np.asarray(L0_attn_wo, dtype=np.float32)
    gate = np.asarray(L0_ffn_gate_weight, dtype=np.float32)
    gup = np.asarray(L0_ffn_gate_up_proj, dtype=np.float32)
    dwn = np.asarray(L0_ffn_down_proj, dtype=np.float32)
    sim = np.ascontiguousarray(np.asarray(L0_ffn_similarity, dtype=np.float32))

    wqT = np.ascontiguousarray(wq.T)
    wkT = np.ascontiguousarray(wk.T)
    wvT = np.ascontiguousarray(wv.T)
    wuT = np.ascontiguousarray(wu.T)
    woT = np.ascontiguousarray(wo.T)
    gateT = np.ascontiguousarray(gate.T)
    wguT = np.ascontiguousarray(gup.transpose(0, 2, 1))
    wdT = np.ascontiguousarray(dwn.transpose(0, 2, 1))

    consts = np.zeros((128, NCONST), dtype=np.float32)
    consts[:, C_ISCALE] = ISCALE
    consts[:, C_SRMS_EPS] = 1e-20
    consts[:, C_NEGH] = -0.5
    consts[:, C_EPS_LN] = 1e-5
    consts[:, C_SRMS_LNC] = 0.5 * math.log(128.0)
    identb = np.eye(128, dtype=np.float32)
    identf = np.eye(128, dtype=np.float32)
    onesf = np.ones((1, 128), dtype=np.float32)
    i32 = np.eye(32, dtype=np.float32)

    key = DEBUG_PHASE
    if key not in _NC_CACHE:
        _NC_CACHE[key] = build_nc(key)
    nc = _NC_CACHE[key]

    xr = x.reshape(8, T, D)
    in_maps = []
    for c in range(8):
        in_maps.append({
            "xin": xr[c], "wqT": wqT, "wkT": wkT, "wvT": wvT, "wuT": wuT,
            "woT": woT, "gateTd": gateT, "wguTd": wguT, "wdTd": wdT,
            "simin": sim, "i32in": i32, "constsin": consts,
            "identbin": identb, "identfin": identf, "onesfin": onesf,
        })
    res = run_bass_kernel_spmd(nc, in_maps, core_ids=list(range(8)))
    outs = np.stack([res.results[c]["out"] for c in range(8)])
    return outs.reshape(16, 2048, 2048)
